# version 03 (final)
# baseline (speedup 1.0000x reference)
import contextlib
import numpy as np
import ml_dtypes
import concourse.bass as bass
import concourse.mybir as mybir
from concourse.bass_utils import run_bass_kernel_spmd

F32 = mybir.dt.float32
BF16 = mybir.dt.bfloat16
FP8 = mybir.dt.float8e5
F32R = mybir.dt.float32r
AL = mybir.AluOpType
ACT = mybir.ActivationFunctionType
AX = mybir.AxisListType

BFnp = ml_dtypes.bfloat16

MAGIC = float(np.float32(1.5 * 2.0 ** 23))
LN1024 = float(np.log(np.float64(1024.0)))
LN127 = float(np.log(np.float64(127.0)))
B_N2EPS = float(np.float32(1024.0 * 1024.0 * 1e-5))

D = 1024
T = 4096
NCH = 8

SUBLAYERS = (1, 2, 3, 4, 5, 6)


def _threefry2x32(k0, k1, x0, x1):
    ks0 = np.uint32(k0); ks1 = np.uint32(k1)
    ks2 = np.uint32(np.uint32(0x1BD11BDA) ^ ks0 ^ ks1)
    x0 = (x0 + ks0).astype(np.uint32); x1 = (x1 + ks1).astype(np.uint32)
    R = [(13, 15, 26, 6), (17, 29, 16, 24)]
    ks = (ks0, ks1, ks2)
    for i in range(5):
        r = R[i % 2]
        for j in range(4):
            x0 = (x0 + x1).astype(np.uint32)
            x1 = ((x1 << np.uint32(r[j])) | (x1 >> np.uint32(32 - r[j]))).astype(np.uint32)
            x1 = x1 ^ x0
        x0 = (x0 + ks[(i + 1) % 3]).astype(np.uint32)
        x1 = (x1 + ks[(i + 2) % 3] + np.uint32(i + 1)).astype(np.uint32)
    return x0, x1


def _dropout_keep_mask(seed, shape, p_keep):
    n = int(np.prod(shape))
    idx = np.arange(n, dtype=np.uint64)
    c1 = (idx >> np.uint64(32)).astype(np.uint32)
    c2 = (idx & np.uint64(0xFFFFFFFF)).astype(np.uint32)
    b1, b2 = _threefry2x32(0, seed, c1, c2)
    bits = b1 ^ b2
    f = ((bits >> np.uint32(9)) | np.uint32(0x3F800000)).view(np.float32) - np.float32(1.0)
    u = np.maximum(f, np.float32(0.0))
    return (u < np.float32(p_keep)).reshape(shape)


def _ts(i, n):
    return slice(i * n, (i + 1) * n)


class Sem:
    def __init__(self, handle, dma):
        self.h = handle
        self.n = 0
        self.step_v = 16 if dma else 1

    def inc(self):
        self.n += self.step_v
        return self.n


class Builder:

    DMA_SEMS = ("sLD", "sGD", "sXP", "sST")

    def __init__(self, nc, stack):
        self.nc = nc
        self.stack = stack
        self.items = []
        self.ready = {}
        self.readers = {}
        self.epoch = ()
        self.sems = {}
        self._semct = 0
        for name, dma in (("sMM", 0), ("sV", 0), ("sS", 0), ("sG", 0)):
            h = stack.enter_context(nc.semaphore(name))
            self.sems[name] = Sem(h, dma)

    def _dma_sem(self, key):
        if key not in self.sems:
            self._semct += 1
            h = self.stack.enter_context(self.nc.semaphore(f"dq{self._semct}"))
            self.sems[key] = Sem(h, 1)
        return self.sems[key]

    def step(self, engine, sem_name, fn, reads=(), writes=(), extra=(), n_dma=1):
        if sem_name in self.DMA_SEMS:
            key = sem_name + ":" + (writes[0] if writes else
                                    (reads[0] if reads else "anon"))
            sem = self._dma_sem(key)
        else:
            sem = self.sems[sem_name]
        waits = {}
        def add(mk):
            if mk is None:
                return
            s, v = mk
            if waits.get(id(s), (None, -1))[1] < v:
                waits[id(s)] = (s, v)
        for r in reads:
            add(self.ready.get(r))
        for w in writes:
            add(self.ready.get(w))
            for mk in self.readers.get(w, ()):
                add(mk)
        for mk in extra:
            add(mk)
        if writes:
            for mk in self.epoch:
                add(mk)
        mark = sem.inc()
        for _ in range(n_dma - 1):
            mark = sem.inc()
        wl = list(waits.values())
        self.items.append((engine, wl, fn, sem))
        mk = (sem, mark)
        for w in writes:
            self.ready[w] = mk
            self.readers[w] = []
        for r in reads:
            self.readers.setdefault(r, []).append(mk)
        return mk

    def new_epoch(self):
        self.epoch = tuple((s, s.n) for s in self.sems.values() if s.n > 0)

    def replay(self, engine, eng):
        for e, wl, fn, sem in self.items:
            if e != engine:
                continue
            for s, v in wl:
                eng.wait_ge(s.h, v)
            self.cur = (sem.h, sem.step_v)
            fn(eng)

    def fin(self, eng):
        sh, sv = self.cur
        return lambda inst: inst.then_inc(sh, sv)


def build_program(sublayers=SUBLAYERS):
    nc = bass.Bass("TRN2", target_bir_lowering=False, debug=False,
                   detect_race_conditions=False)
    st = contextlib.ExitStack()

    def din(name, shape, dt=F32):
        return nc.dram_tensor(name, list(shape), dt, kind="ExternalInput").ap()

    x_in = din("x_fm", [128, NCH * T])
    mask_in = din("mask_fm", [128, NCH * T], BF16)
    flag_in = din("flag", [128, 1])
    tri_in = din("tri", [128, 128], BF16)
    idf_in = din("ident_f", [128, 128])
    ind_in = din("ind", [128, NCH * 16])
    indT_in = din("indT", [16, NCH * 128])
    W_qkv = din("L0_attn_W_qkv", [1024, 3072])
    b_qkv = din("b_qkv_fm", [128, 24])
    b_qkv_row_in = din("b_qkv_row", [1, 3072])
    W_out = din("L0_attn_W_out", [1024, 1024])
    b_out = din("b_out_fm", [128, 8])
    W1 = din("W1_T", [1024, 4096])
    b1 = din("b1_fm", [128, 32])
    W2 = din("W2_T", [4096, 1024])
    b2 = din("b2_fm", [128, 8])
    Wq = din("L1_attn_Wq", [1024, 1024])
    Wk = din("L1_attn_Wk", [1024, 1024])
    Wv = din("L1_attn_Wv", [1024, 1024])
    Wg = din("L1_attn_Wg", [1024, 1024])
    Wo = din("L1_attn_Wo", [1024, 1024])
    gam = din("gamma_fm", [128, 8])
    bet = din("beta_fm", [128, 8])
    w1L = din("w1L_T", [1024, 4096])
    b1L = din("b1L_fm", [128, 32])
    w2L = din("w2L_T", [4096, 1024])
    b2L = din("b2L_fm", [128, 8])
    out_d = nc.dram_tensor("out_fm", [128, NCH * T], F32, kind="ExternalOutput").ap()
    dbg_d = (nc.dram_tensor("dbg_fm", [128, 8192], F32, kind="ExternalOutput").ap()
             if any(s > 100 for s in sublayers) else None)

    NW, WTOK = 16, 256
    sp_q = nc.dram_tensor("sp_q", [128, NCH * T], BF16, kind="Internal").ap()
    sp_k = nc.dram_tensor("sp_k", [128, NCH * T], BF16, kind="Internal").ap()
    sp_v = nc.dram_tensor("sp_v", [128, NW * 2 * 1040], BF16, kind="Internal").ap()
    sp_kt = nc.dram_tensor("sp_kt", [128, NCH * T], BF16, kind="Internal").ap()
    cc_in = nc.dram_tensor("cc_in", [64, 1040], F32, kind="Internal").ap()
    cc_out = nc.dram_tensor("cc_out", [64, 1040], F32, kind="Internal").ap()
    ar_in = nc.dram_tensor("ar_in", [128, 8], F32, kind="Internal").ap()
    ar_out = nc.dram_tensor("ar_out", [128, 8], F32, kind="Internal").ap()
    sp_h8 = nc.dram_tensor("sp_h8", [128, 32 * T], FP8, kind="Internal").ap()
    sp_k2 = nc.dram_tensor("sp_k2", [128, 32 * T], BF16, kind="Internal").ap()
    sp_r2i = nc.dram_tensor("sp_r2i", [128, T], F32, kind="Internal").ap()
    sp_w1q = nc.dram_tensor("sp_w1q", [128, 8 * 4096], FP8, kind="Internal").ap()
    sp_w2q = nc.dram_tensor("sp_w2q", [128, 32 * 1024], FP8, kind="Internal").ap()
    RG = [[0, 1], [2, 3], [4, 5], [6, 7]]

    def sbuf(name, shape, dt):
        return st.enter_context(nc.sbuf_tensor(name, list(shape), dt))

    x_sb = sbuf("x_sb", [128, NCH * T], F32)
    ones_b = sbuf("ones_b", [128, 128], BF16)
    ones_f = sbuf("ones_f", [1, 128], F32)
    tri_sb = sbuf("tri_sb", [128, 128], BF16)
    idf_sb = sbuf("idf_sb", [128, 128], F32)
    ind_b = sbuf("ind_b", [128, NCH * 16], BF16)
    flag_sb = sbuf("flag_sb", [128, 1], F32)
    bqkv_sb = sbuf("bqkv_sb", [128, 24], F32)
    bout_sb = sbuf("bout_sb", [128, 8], F32)
    b1q_sb = sbuf("b1q_sb", [128, 32], F32)
    b2q_sb = sbuf("b2q_sb", [128, 8], F32)
    b1L_sb = sbuf("b1L_sb", [128, 32], F32)
    b2L_sb = sbuf("b2L_sb", [128, 8], F32)
    gam_sb = sbuf("gam_sb", [128, 8], F32)
    bet_sb = sbuf("bet_sb", [128, 8], F32)
    t8a = sbuf("t8a", [128, 32], FP8)
    mw_sb = sbuf("mw_sb", [128, 2], F32)
    ccol = sbuf("ccol", [128, 8], F32)
    pm_col = sbuf("pm_col", [128, 4], F32)
    pm_row = sbuf("pm_row", [1, 512], F32)
    xb_g = [sbuf("xb_g0", [128, 512], BF16), sbuf("xb_g1", [128, 512], BF16)]
    sqb_g = [sbuf("sqb_g0", [128, 512], BF16), sbuf("sqb_g1", [128, 512], BF16)]
    sw_sb = sbuf("sw_sb", [128, 2], F32)
    wsa = sbuf("wsa", [128, 12], F32)
    ident_b = sbuf("ident_b", [128, 128], BF16)
    fli_sb = sbuf("fli_sb", [128, 1], F32)
    indtb = sbuf("indtb", [16, NCH * 128], BF16)
    ks_sb = sbuf("ks_sb", [128, 8], F32)

    dbg_f = sbuf("dbgf", [128, NCH * 512], F32) if any(s > 100 for s in sublayers) else None
    AR_COLS = (nc.sbuf_bytes_remaining - 64) // 4 // 8 * 8
    arena = sbuf("arena", [128, AR_COLS], F32)
    stage = arena[:, 0:128]

    class Carver:
        def __init__(self):
            self.off = 0

        def take(self, nbytes):
            o = self.off
            assert o + nbytes <= AR_COLS * 4, (o, nbytes)
            self.off += (nbytes + 31) // 32 * 32
            return o

        def f32(self, cols):
            o = self.take(cols * 4)
            return arena[:, o // 4: o // 4 + cols]

        def bf16(self, cols):
            o = self.take(cols * 2)
            return arena[:].bitcast(BF16)[:, o // 2: o // 2 + cols]

        def fp8(self, cols):
            o = self.take(cols)
            return arena[:].bitcast(FP8)[:, o: o + cols]

        def f32r(self, cols):
            o = self.take(cols * 4)
            return arena[:].bitcast(F32R)[:, o // 4: o // 4 + cols]

    ps = [st.enter_context(nc.psum_tensor(f"ps{i}", [128, 512], F32)) for i in range(8)]

    B = Builder(nc, st)
    FIN = B.fin

    def fin(e, inst):
        sh, sv = B.cur
        inst.then_inc(sh, sv)

    def pre_sync(e):
        sh = B.cur[0]
        e.dma_start(out=x_sb[:], in_=x_in[:]).then_inc(sh, 16)
        e.dma_start(out=tri_sb[:], in_=tri_in[:]).then_inc(sh, 16)
        e.dma_start(out=idf_sb[:], in_=idf_in[:]).then_inc(sh, 16)
        e.dma_start(out=flag_sb[:], in_=flag_in[:]).then_inc(sh, 16)
        e.dma_start(out=bqkv_sb[:], in_=b_qkv[:]).then_inc(sh, 16)
        e.dma_start(out=bout_sb[:], in_=b_out[:]).then_inc(sh, 16)
        e.dma_start(out=b1L_sb[:], in_=b1L[:]).then_inc(sh, 16)
        e.dma_start(out=b2L_sb[:], in_=b2L[:]).then_inc(sh, 16)
        e.dma_start(out=gam_sb[:], in_=gam[:]).then_inc(sh, 16)
        fin(e, e.dma_start(out=bet_sb[:], in_=bet[:]))
    B.step("sync", "sLD", pre_sync, n_dma=10,
           writes=("x", "tri", "flag", "bqkv", "bout", "b1L", "b2L", "gam", "bet", "ident"))

    def pre_g(e):
        sh = B.cur[0]
        e.dma_start(out=ind_b[:], in_=ind_in[:]).then_inc(sh, 16)
        e.dma_start(out=indtb[:], in_=indT_in[:]).then_inc(sh, 16)
        fin(e, e.dma_start(out=t8a[:, 0:32], in_=b1[:]))
    B.step("gpsimd", "sGD", pre_g, n_dma=3, writes=("ind", "indtb", "b1q8"))

    def pre_v(e):
        e.memset(ones_b[:], 1.0)
        e.memset(ones_f[:], 1.0)
        e.tensor_copy(ident_b[:], idf_sb[:])
        e.tensor_scalar(fli_sb[:], flag_sb[:], -1.0, 1.0, AL.mult, AL.add)
        e.memset(ccol[:, 0:1], B_N2EPS)
        e.memset(ccol[:, 1:2], LN1024)
        e.memset(ccol[:, 2:3], 1e-5)
        e.memset(ccol[:, 3:4], -LN127)
        e.memset(ccol[:, 4:5], LN127)
        e.memset(ccol[:, 5:6], float(np.float32(4096.0 * 1e-5)))
        e.memset(ccol[:, 6:7], float(np.log(np.float64(4096.0)) * 0.5))
        e.memset(ccol[:, 7:8], 1e-6)
        e.tensor_copy(b1q_sb[:], t8a[:, 0:32])
        fin(e, e.engine_nop())
    B.step("vector", "sV", pre_v, reads=("b1q8", "ident", "flag"),
           writes=("ones", "b1q", "identb", "fli"))

    def pre_g2(e):
        fin(e, e.dma_start(out=t8a[:, 0:8], in_=b2[:]))
    B.step("gpsimd", "sGD", pre_g2, reads=("b1q",), writes=("b2q8",))

    def pre_v2(e):
        e.tensor_copy(b2q_sb[:], t8a[:, 0:8])
        fin(e, e.engine_nop())
    B.step("vector", "sV", pre_v2, reads=("b2q8",), writes=("b2q",))


    def emit_pmax_bcast(bufname, buf, tau, op, dst_psname, dst_ps):
        nblk = tau // 128
        for j in range(nblk):
            B.step("tensor", "sMM",
                   lambda e, j=j: fin(e, e.transpose(
                       ps[7][:, 0:128], buf[0:128, _ts(j, 128)], idf_sb[:])),
                   reads=(bufname, "ident"), writes=("ps7",))
            B.step("vector", "sV",
                   lambda e, j=j: fin(e, e.tensor_reduce(
                       pm_col[:, j:j + 1], ps[7][:, 0:128], AX.X, op)),
                   reads=("ps7",), writes=("pmcol",))
        for j in range(nblk):
            B.step("tensor", "sMM",
                   lambda e, j=j: fin(e, e.transpose(
                       ps[7][0:1, 0:128], pm_col[:, j:j + 1], idf_sb[:])),
                   reads=("pmcol", "ident"), writes=("ps7",))
            B.step("vector", "sV",
                   lambda e, j=j: fin(e, e.tensor_copy(
                       pm_row[0:1, _ts(j, 128)], ps[7][0:1, 0:128])),
                   reads=("ps7",), writes=("pmrow",))
        B.step("tensor", "sMM",
               lambda e: fin(e, e.matmul(dst_ps[:, 0:tau], ones_f[0:1, :],
                                         pm_row[0:1, 0:tau], start=True, stop=True)),
               reads=("pmrow", "ones"), writes=(dst_psname,))

    def emit_wstats(C):
        B.new_epoch()
        scr8 = C.f32(4096)
        wb = [C.f32(4096), C.f32(4096)]
        B.step("vector", "sV", lambda e: fin(e, e.memset(wsa[:, 0:2], 0.0)),
               writes=("wsa",))
        pend = [None]
        kslices = [(wi, wsrc, sl, ncols)
                   for wi, wsrc, nsl, ncols in ((0, w1L, 8, 4096), (1, w2L, 32, 1024))
                   for sl in range(nsl)]
        for gi, (wi, wsrc, sl, ncols) in enumerate(kslices):
            buf = wb[gi % 2]
            res = f"wstb{gi % 2}"
            B.step("sync", "sLD",
                   lambda e, b=buf, w=wsrc, s=sl, nc_=ncols: fin(e, e.dma_start(
                       out=b[:, 0:nc_], in_=w.tensor.ap()[_ts(s, 128), :])),
                   writes=(res,))

            def f_add(e, b=buf, nc_=ncols, slot=2 + gi % 2, pd=pend[0]):
                s8 = scr8[:, 0:nc_]
                bb = b[:, 0:nc_]
                e.tensor_scalar(s8, bb, -1.0, None, AL.mult)
                e.tensor_tensor(s8, s8, bb, AL.max)
                if pd is not None:
                    pwi, pslot = pd
                    e.tensor_tensor(wsa[:, pwi:pwi + 1], wsa[:, pwi:pwi + 1],
                                    wsa[:, pslot:pslot + 1], AL.add)
                fin(e, e.tensor_reduce(wsa[:, slot:slot + 1], s8, AX.X, AL.add))
            B.step("vector", "sV", f_add, reads=(res, "wsa"), writes=("wsa", res))
            pend[0] = (wi, 2 + gi % 2)

        def f_drain(e, pd=pend[0]):
            e.tensor_scalar(scr8[:], scr8[:], 1.0, None, AL.mult)
            e.tensor_scalar(scr8[:], scr8[:], 1.0, None, AL.mult)
            pwi, pslot = pd
            fin(e, e.tensor_tensor(wsa[:, pwi:pwi + 1], wsa[:, pwi:pwi + 1],
                                   wsa[:, pslot:pslot + 1], AL.add))
        B.step("vector", "sV", f_drain, reads=("wsa",), writes=("wsa",))

        for c in (0, 1):
            B.step("tensor", "sMM",
                   lambda e, c=c: fin(e, e.transpose(
                       ps[7][0:1, 0:128], wsa[:, c:c + 1], idf_sb[:])),
                   reads=("wsa", "ident"), writes=("ps7",))
            B.step("vector", "sV",
                   lambda e, c=c: fin(e, e.tensor_reduce(
                       wsa[0:1, c:c + 1], ps[7][0:1, 0:128], AX.X, AL.add)),
                   reads=("ps7",), writes=("wsarow",) if c == 1 else ())

        def f_bc(e):
            fin(e, e.matmul(ps[4][:, 0:2], ones_f[0:1, :], wsa[0:1, 0:2],
                            start=True, stop=True))
        B.step("tensor", "sMM", f_bc, reads=("wsarow", "ones"), writes=("ps4",))

        def f_fin(e):
            e.tensor_scalar(mw_sb[:], ps[4][:, 0:2], 1.0 / 4194304.0, 1e-5,
                            AL.mult, AL.max)
            fin(e, e.engine_nop())
        B.step("vector", "sV", f_fin, reads=("ps4",), writes=("mw",))

        def f_sw(e):
            e.activation(sw_sb[:], mw_sb[:], ACT.Ln)
            fin(e, e.activation(sw_sb[:], sw_sb[:], ACT.Exp, scale=-1.0))
        B.step("scalar", "sS", f_sw, reads=("mw",), writes=("sw",))

    def mk_stats_bufs(C, tau, want_r=True):
        d = dict(
            stA=C.f32(tau), stB=C.f32(tau),
            m_t=C.f32(tau), s_t=C.f32(tau),
        )
        if want_r:
            d["r_t"] = C.f32(tau)
        return d

    def emit_stats(SB, c0, tau, want_r=False, src=None, srcres="x"):
        for c in range(NCH):
            xs = src(c) if src is not None else x_sb[:, c * T + c0: c * T + c0 + tau]
            xr = xb_g[c % 2][:, 0:tau]
            sr = sqb_g[c % 2][:, 0:tau]
            B.step("vector", "sV",
                   lambda e, xs=xs, xr=xr: fin(e, e.tensor_copy(xr, xs)),
                   reads=(srcres,), writes=(f"xr{c % 2}",))
            B.step("scalar", "sS",
                   lambda e, xs=xs, sr=sr: fin(e, e.activation(sr, xs, ACT.Square)),
                   reads=(srcres,), writes=(f"sqr{c % 2}",))

            def f_mm(e, c=c, xr=xr, sr=sr, tau=tau):
                e.matmul(ps[2][:, 0:tau], ones_b[:], xr,
                         start=(c == 0), stop=(c == NCH - 1))
                fin(e, e.matmul(ps[3][:, 0:tau], ones_b[:], sr,
                                start=(c == 0), stop=(c == NCH - 1)))
            B.step("tensor", "sMM", f_mm,
                   reads=(f"xr{c % 2}", f"sqr{c % 2}", "ones"),
                   writes=("ps2", "ps3") if c == 0 else ())
        mm_done = (B.sems["sMM"], B.sems["sMM"].n)
        B.ready["ps2"] = mm_done
        B.ready["ps3"] = mm_done
        B.readers["ps2"] = []
        B.readers["ps3"] = []

        stA, stB = SB["stA"][:, 0:tau], SB["stB"][:, 0:tau]
        mdst, sdst = SB["m_t"][:, 0:tau], SB["s_t"][:, 0:tau]

        def f_fin1(e):
            e.tensor_scalar(mdst, ps[2][:, 0:tau], 1.0 / 1024.0, None, AL.mult)
            e.tensor_tensor(stA, mdst, mdst, AL.mult)
            fin(e, e.scalar_tensor_tensor(stA, ps[3][:, 0:tau], 1.0 / 1024.0, stA,
                                          AL.mult, AL.subtract))
        B.step("vector", "sV", f_fin1, reads=("ps2", "ps3", "stA"),
               writes=("m_t", "stA"))

        def f_fin2(e):
            e.activation(stB, stA, ACT.Ln, bias=ccol[:, 2:3])
            fin(e, e.activation(sdst, stB, ACT.Exp, scale=-0.5))
        B.step("scalar", "sS", f_fin2, reads=("stA", "stB"), writes=("s_t", "stB"))

        if want_r:
            r_t = SB["r_t"][:, 0:tau]
            B.step("scalar", "sS",
                   lambda e: fin(e, e.activation(stA, stA, ACT.Ln)),
                   reads=("stA",), writes=("stA",), extra=(B.ready.get("s_t"),))
            B.step("vector", "sV",
                   lambda e: fin(e, e.tensor_tensor(stA, stA, stB, AL.subtract)),
                   reads=("stA", "stB"), writes=("stA",))

            def f_r3(e):
                e.activation(stA, stA, ACT.Exp)
                e.activation(stB, stA, ACT.Ln, bias=ccol[:, 2:3])
                fin(e, e.activation(r_t, stB, ACT.Exp, scale=-0.5))
            B.step("scalar", "sS", f_r3, reads=("stA", "stB"),
                   writes=("r_t", "stA", "stB"))

    def emit_l0ffn(tag, phases=(1, 2)):
        TAU = 512
        NT = T // TAU
        if tag == "dbg":
            NT = 1
        B.new_epoch()
        C = Carver()
        wres = C.fp8(8 * 4096)
        SB = mk_stats_bufs(C, TAU, want_r=False)
        xq8 = C.fp8(NCH * TAU)
        hq8t = C.fp8(32 * TAU) if tag != "dbg" else None
        hrot = [C.bf16(TAU), C.bf16(TAU)]
        dt1 = C.f32(TAU)

        B.step("gpsimd", "sGD",
               lambda e: fin(e, e.dma_start(
                   out=wres.rearrange("p (c f) -> p c f", c=NCH),
                   in_=W1.tensor.ap().rearrange("(c p) f -> p c f", p=128))),
               writes=("wres",))

        for ti in range(NT):
            c0 = ti * TAU
            emit_stats(SB, c0, TAU)
            for c in range(NCH):
                xs = x_sb[:, c * T + c0: c * T + c0 + TAU]

                def f_xq(e, xs=xs, c=c):
                    e.tensor_tensor(dt1, xs, SB["m_t"][:, 0:TAU], AL.subtract)
                    fin(e, e.tensor_tensor(xq8[:, _ts(c, TAU)], dt1,
                                           SB["s_t"][:, 0:TAU], AL.mult))
                B.step("vector", "sV", f_xq, reads=("x", "m_t", "s_t", "dt1"),
                       writes=("xq8", "dt1") if c == 0 else ("dt1",))
            xq_done = (B.sems["sV"], B.sems["sV"].n)
            B.ready["xq8"] = xq_done

            if tag == "dbg" and ti == 0:
                B.step("sync", "sST",
                       lambda e: fin(e, e.dma_start(out=dbg_d[:, 0:TAU], in_=SB["m_t"][:, 0:TAU])),
                       reads=("m_t",), writes=("dbgo",))
                B.step("sync", "sST",
                       lambda e: fin(e, e.dma_start(out=dbg_d[:, TAU:2*TAU], in_=SB["s_t"][:, 0:TAU])),
                       reads=("s_t",), writes=("dbgo",))
                B.step("vector", "sV",
                       lambda e: fin(e, e.tensor_copy(d_dbg_f32(), xq8[:])),
                       reads=("xq8",), writes=("dbg1",))
                B.step("sync", "sST",
                       lambda e: fin(e, e.dma_start(out=dbg_d[:, 2*TAU:2*TAU+NCH*TAU], in_=d_dbg_f32())),
                       reads=("dbg1",), writes=("dbgo",))
                B.step("vector", "sV",
                       lambda e: fin(e, e.tensor_copy(d_dbg_f32()[:, 0:128], ones_b[:])),
                       reads=("ones",), writes=("dbg1",))
                B.step("vector", "sV",
                       lambda e: fin(e, e.tensor_copy(d_dbg_f32()[:, 128:640], xb_g[1][:])),
                       reads=("xr1",), writes=("dbg1",))
                B.step("sync", "sST",
                       lambda e: fin(e, e.dma_start(out=dbg_d[:, 2*TAU+NCH*TAU:2*TAU+NCH*TAU+640], in_=d_dbg_f32()[:, 0:640])),
                       reads=("dbg1",), writes=("dbgo",))
                return

            for F in range(32):
                sl = F % 2

                def f_mm(e, F=F, sl=sl):
                    for k in range(8):
                        inst = e.matmul(ps[sl][:, 0:TAU],
                                        wres[:, k * 4096 + F * 128: k * 4096 + F * 128 + 128],
                                        xq8[:, _ts(k, TAU)], start=(k == 0), stop=(k == 7))
                    fin(e, inst)
                B.step("tensor", "sMM", f_mm, reads=("wres", "xq8"),
                       writes=(f"ps{sl}",))

                def f_ev(e, F=F, sl=sl):
                    fin(e, e.activation(hrot[sl], ps[sl][:, 0:TAU], ACT.Relu,
                                        bias=b1q_sb[:, F:F + 1]))
                B.step("scalar", "sS", f_ev, reads=(f"ps{sl}", "b1q"),
                       writes=(f"hrot{sl}",))

                def f_c8(e, F=F, sl=sl):
                    fin(e, e.tensor_copy(hq8t[:, _ts(F, TAU)], hrot[sl]))
                B.step("vector", "sV", f_c8, reads=(f"hrot{sl}",),
                       writes=("hq8t",) if F == 0 else ())
            hq_done = (B.sems["sV"], B.sems["sV"].n)
            B.ready["hq8t"] = hq_done

            B.step("sync", "sST",
                   lambda e, ti=ti: fin(e, e.dma_start(
                       out=sp_h8[:, ti * 32 * TAU: (ti + 1) * 32 * TAU], in_=hq8t)),
                   reads=("hq8t",), writes=("sp_h8",))

        if 2 not in phases:
            return
        B.new_epoch()
        C2 = Carver()
        wres2 = C2.fp8(32 * 1024)
        hq8t2 = C2.fp8(32 * TAU)
        mask_t = [C2.bf16(TAU), C2.bf16(TAU)]
        yrot = [C2.bf16(TAU), C2.bf16(TAU)]
        evt = C2.f32(TAU)

        B.step("gpsimd", "sGD",
               lambda e: fin(e, e.dma_start(
                   out=wres2.rearrange("p (c f) -> p c f", c=32),
                   in_=W2.tensor.ap().rearrange("(c p) f -> p c f", p=128))),
               writes=("wres",), extra=(B.ready.get("hq8t"),))

        for ti in range(NT):
            c0 = ti * TAU
            h_done = B.step("sync", "sLD",
                   lambda e, ti=ti: fin(e, e.dma_start(
                       out=hq8t2, in_=sp_h8[:, ti * 32 * TAU: (ti + 1) * 32 * TAU])),
                   reads=("sp_h8",), writes=("hq8t2",))

            for G in range(8):
                sl = G % 2
                B.step("sync", "sLD",
                       lambda e, G=G, c0=c0, sl=sl: fin(e, e.dma_start(
                           out=mask_t[sl],
                           in_=mask_in[:, G * T + c0: G * T + c0 + TAU])),
                       writes=(f"mask_{sl}",))

                def f_mm(e, G=G, sl=sl):
                    for k in range(32):
                        inst = e.matmul(ps[4 + sl][:, 0:TAU],
                                        wres2[:, k * 1024 + G * 128: k * 1024 + G * 128 + 128],
                                        hq8t2[:, _ts(k, TAU)], start=(k == 0), stop=(k == 31))
                    fin(e, inst)
                B.step("tensor", "sMM", f_mm, reads=("wres", "hq8t2"),
                       writes=(f"ps{4 + sl}",))

                def f_ev(e, G=G, sl=sl):
                    fin(e, e.activation(yrot[sl], ps[4 + sl][:, 0:TAU], ACT.Identity,
                                        bias=b2q_sb[:, G:G + 1]))
                B.step("scalar", "sS", f_ev, reads=(f"ps{4 + sl}", "b2q"),
                       writes=(f"yrot{sl}",))

                xs = x_sb[:, G * T + c0: G * T + c0 + TAU]

                def f_res(e, G=G, sl=sl, xs=xs):
                    e.tensor_tensor(evt, yrot[sl], mask_t[sl], AL.mult)
                    fin(e, e.tensor_tensor(xs, xs, evt, AL.add))
                B.step("vector", "sV", f_res,
                       reads=(f"yrot{sl}", f"mask_{sl}", "evt"), writes=("x", "evt"))

    def emit_wquant():
        B.new_epoch()
        C = Carver()
        wld = [C.f32(4096), C.f32(4096)]
        wq8 = [C.fp8(4096), C.fp8(4096)]
        for wi, wsrc, dst, nsl, ncols in ((0, w1L, sp_w1q, 8, 4096),
                                          (1, w2L, sp_w2q, 32, 1024)):
            for sl in range(nsl):
                bi = sl % 2
                B.step("sync", "sLD",
                       lambda e, b=wld[bi], w=wsrc, s=sl, nc_=ncols: fin(
                           e, e.dma_start(out=b[:, 0:nc_],
                                          in_=w.tensor.ap()[_ts(s, 128), :])),
                       writes=(f"wld_{bi}",))

                def f_q(e, bi=bi, wi=wi, nc_=ncols):
                    b = wld[bi][:, 0:nc_]
                    q = wq8[bi][:, 0:nc_]
                    e.tensor_scalar(b, b, sw_sb[:, wi:wi + 1], None, AL.mult)
                    e.tensor_scalar(b, b, MAGIC, MAGIC, AL.add, AL.subtract)
                    fin(e, e.tensor_scalar(q, b, 1.0, -1.0, AL.min, AL.max))
                B.step("vector", "sV", f_q, reads=(f"wld_{bi}", "sw"),
                       writes=(f"wld_{bi}", f"wq8_{bi}"))

                B.step("sync", "sST",
                       lambda e, bi=bi, d=dst, s=sl, nc_=ncols: fin(
                           e, e.dma_start(out=d[:, s * nc_:(s + 1) * nc_],
                                          in_=wq8[bi][:, 0:nc_])),
                       reads=(f"wq8_{bi}",), writes=("spwq",))

    def emit_l1ffn(tag):
        TAU = 256
        NT = T // TAU
        B.new_epoch()
        C = Carver()
        wres = C.fp8(8 * 4096)
        SB = mk_stats_bufs(C, TAU, want_r=True)
        kx = C.bf16(NCH * TAU)
        hbf = C.bf16(32 * TAU)
        hmax = C.bf16(TAU)
        hmin = C.bf16(TAU)
        rho = C.f32(TAU)
        rt1 = C.f32(TAU)
        rho2i = C.f32(TAU)
        srm = C.f32(TAU)
        evt = C.f32(TAU)
        evt2 = C.f32(TAU)
        dt1 = C.f32(TAU)

        B.step("sync", "sLD",
               lambda e: fin(e, e.dma_start(out=wres, in_=sp_w1q[:])),
               reads=("spwq",), writes=("wres",))

        for ti in range(NT):
            c0 = ti * TAU
            emit_stats(SB, c0, TAU, want_r=True)

            def f_mx(e, c0=c0):
                src3 = x_sb[:].rearrange("p (c t) -> p c t", c=NCH)[:, :, c0:c0 + TAU]
                src3 = src3.transpose([0, 2, 1])
                e.tensor_reduce(evt, src3, AX.X, AL.max)
                fin(e, e.tensor_reduce(evt2, src3, AX.X, AL.min))
            B.step("vector", "sV", f_mx, reads=("x", "evt", "evt2"),
                   writes=("evt", "evt2"))

            emit_pmax_bcast("evt", evt, TAU, AL.max, "ps6", ps[6])

            def f_mxa(e):
                fin(e, e.tensor_tensor(srm, ps[6][:, 0:TAU], SB["m_t"][:, 0:TAU],
                                       AL.subtract))
            B.step("vector", "sV", f_mxa, reads=("ps6", "m_t", "srm"), writes=("srm",))

            emit_pmax_bcast("evt2", evt2, TAU, AL.min, "ps6", ps[6])

            def f_mxc(e):
                e.scalar_tensor_tensor(SB["stA"][:, 0:TAU], ps[6][:, 0:TAU], -1.0,
                                       SB["m_t"][:, 0:TAU], AL.mult, AL.add)
                e.tensor_tensor(SB["stA"][:, 0:TAU], SB["stA"][:, 0:TAU], srm, AL.max)
                e.tensor_tensor(srm, SB["s_t"][:, 0:TAU], SB["r_t"][:, 0:TAU], AL.mult)
                e.tensor_tensor(SB["stA"][:, 0:TAU], SB["stA"][:, 0:TAU], srm, AL.mult)
                fin(e, e.tensor_scalar(SB["stA"][:, 0:TAU], SB["stA"][:, 0:TAU],
                                       1e-5, None, AL.max))
            B.step("vector", "sV", f_mxc, reads=("ps6", "s_t", "r_t", "stA", "srm", "m_t"),
                   writes=("stA", "srm"))

            def f_sc(e):
                e.activation(evt, SB["stA"][:, 0:TAU], ACT.Ln)
                e.activation(evt2, evt, ACT.Exp, scale=-1.0, bias=ccol[:, 4:5])
                fin(e, e.activation(evt, evt, ACT.Exp, bias=ccol[:, 3:4]))
            B.step("scalar", "sS", f_sc, reads=("stA", "evt", "evt2"),
                   writes=("evt", "evt2"))

            def f_rho(e):
                e.tensor_tensor(rho, srm, evt2, AL.mult)
                fin(e, e.tensor_scalar(rt1, evt, mw_sb[:, 0:1], None, AL.mult))
            B.step("vector", "sV", f_rho, reads=("srm", "evt", "evt2", "rho", "rt1", "mw"),
                   writes=("rho", "rt1"))

            for c in range(NCH):
                xs = x_sb[:, c * T + c0: c * T + c0 + TAU]

                def f_q(e, xs=xs, c=c):
                    e.tensor_tensor(dt1, xs, SB["m_t"][:, 0:TAU], AL.subtract)
                    e.tensor_tensor(dt1, dt1, rho, AL.mult)
                    fin(e, e.tensor_scalar(kx[:, _ts(c, TAU)], dt1, MAGIC, MAGIC,
                                           AL.add, AL.subtract))
                B.step("vector", "sV", f_q, reads=("x", "m_t", "rho", "dt1"),
                       writes=("kx", "dt1") if c == 0 else ("dt1",))
            kx_done = (B.sems["sV"], B.sems["sV"].n)
            B.ready["kx"] = kx_done

            for F in range(32):
                sl = F % 2

                def f_mm(e, F=F, sl=sl):
                    for k in range(8):
                        inst = e.matmul(ps[sl][:, 0:TAU],
                                        wres[:, k * 4096 + F * 128: k * 4096 + F * 128 + 128],
                                        kx[:, _ts(k, TAU)], start=(k == 0), stop=(k == 7))
                    fin(e, inst)
                B.step("tensor", "sMM", f_mm, reads=("wres", "kx"), writes=(f"ps{sl}",))

                def f_sc1(e, F=F, sl=sl):
                    fin(e, e.tensor_tensor(dt1, ps[sl][:, 0:TAU], rt1, AL.mult))
                B.step("vector", "sV", f_sc1, reads=(f"ps{sl}", "rt1", "dt1"),
                       writes=("dt1",))
                ps_free = (B.sems["sV"], B.sems["sV"].n)
                B.ready[f"ps{sl}"] = B.ready[f"ps{sl}"]
                B.readers[f"ps{sl}"].append(ps_free)

                def f_ev(e, F=F, sl=sl):
                    fin(e, e.activation(hbf[:, _ts(F, TAU)], dt1, ACT.Gelu,
                                        bias=b1L_sb[:, F:F + 1]))
                B.step("scalar", "sS", f_ev, reads=("dt1", "b1L"),
                       writes=(("hbf",) if F == 0 else ()) + (f"hbfF{sl}",),
                       extra=(B.ready.get("hbf"),) if F == 0 else ())

                def f_sq(e, F=F, sl=sl):
                    fin(e, e.activation(sqb_g[sl][:, 0:TAU], hbf[:, _ts(F, TAU)],
                                        ACT.Square))
                B.step("scalar", "sS", f_sq, reads=(f"hbfF{sl}",), writes=(f"sqr{sl}",))

                def f_hmm(e, F=F, sl=sl):
                    fin(e, e.matmul(ps[3][:, 0:TAU], ones_b[:], sqb_g[sl][:, 0:TAU],
                                    start=(F == 0), stop=(F == 31)))
                B.step("tensor", "sMM", f_hmm, reads=(f"sqr{sl}", "ones"),
                       writes=("ps3",) if F == 0 else ())
                if F == 31:
                    mmd = (B.sems["sMM"], B.sems["sMM"].n)
                    B.ready["ps3"] = mmd
                    B.readers["ps3"] = []

                def f_hex(e, F=F):
                    if F == 0:
                        e.tensor_copy(hmax, hbf[:, _ts(F, TAU)])
                        fin(e, e.tensor_copy(hmin, hbf[:, _ts(F, TAU)]))
                    else:
                        e.tensor_tensor(hmax, hmax, hbf[:, _ts(F, TAU)], AL.max)
                        fin(e, e.tensor_tensor(hmin, hmin, hbf[:, _ts(F, TAU)], AL.min))
                B.step("vector", "sV", f_hex, reads=("hmax", "hmin", f"hbfF{sl}"),
                       writes=("hmax", "hmin"))

            def f_tree(e):
                e.tensor_scalar(hmin[:, :], hmin[:, :], -1.0, None, AL.mult)
                e.tensor_tensor(hmax[:, :], hmax[:, :], hmin[:, :], AL.max)
                fin(e, e.tensor_copy(evt[:, 0:TAU], hmax[:, :]))
            B.step("vector", "sV", f_tree, reads=("hmax", "hmin", "evt"),
                   writes=("hmax", "hmin", "evt"))

            emit_pmax_bcast("evt", evt, TAU, AL.max, "ps6", ps[6])

            def f_r2(e):
                e.activation(SB["stB"][:, 0:TAU], ps[3][:, 0:TAU], ACT.Ln,
                             bias=ccol[:, 5:6])
                fin(e, e.activation(SB["stB"][:, 0:TAU], SB["stB"][:, 0:TAU], ACT.Exp,
                                    scale=-0.5, bias=ccol[:, 6:7]))
            B.step("scalar", "sS", f_r2, reads=("ps3", "stB"), writes=("stB",))

            def f_mx2(e):
                e.tensor_tensor(SB["stA"][:, 0:TAU], ps[6][:, 0:TAU],
                                SB["stB"][:, 0:TAU], AL.mult)
                fin(e, e.tensor_scalar(SB["stA"][:, 0:TAU], SB["stA"][:, 0:TAU],
                                       1e-5, None, AL.max))
            B.step("vector", "sV", f_mx2, reads=("ps6", "stB", "stA"), writes=("stA",))

            def f_sc2(e):
                e.activation(evt, SB["stA"][:, 0:TAU], ACT.Ln)
                e.activation(rho2i, evt, ACT.Exp, bias=ccol[:, 3:4])
                fin(e, e.activation(evt2, evt, ACT.Exp, scale=-1.0, bias=ccol[:, 4:5]))
            B.step("scalar", "sS", f_sc2, reads=("stA", "evt", "evt2", "rho2i"),
                   writes=("rho2i", "evt", "evt2"))

            def f_rr(e):
                fin(e, e.tensor_tensor(evt, SB["stB"][:, 0:TAU], evt2, AL.mult))
            B.step("vector", "sV", f_rr, reads=("stB", "evt2", "evt"), writes=("evt",))

            for F in range(32):
                def f_q2(e, F=F):
                    e.tensor_tensor(dt1, hbf[:, _ts(F, TAU)], evt, AL.mult)
                    fin(e, e.tensor_scalar(hbf[:, _ts(F, TAU)], dt1, MAGIC, MAGIC,
                                           AL.add, AL.subtract))
                B.step("vector", "sV", f_q2, reads=("hbf", "evt", "dt1"),
                       writes=("dt1", "hbf") if F == 31 else ("dt1",))
            k2_done = (B.sems["sV"], B.sems["sV"].n)
            B.ready["hbf"] = k2_done

            B.step("sync", "sST",
                   lambda e, ti=ti: fin(e, e.dma_start(
                       out=sp_k2[:, ti * 32 * TAU: (ti + 1) * 32 * TAU], in_=hbf)),
                   reads=("hbf",), writes=("sp_k2",))
            B.step("sync", "sST",
                   lambda e, ti=ti: fin(e, e.dma_start(
                       out=sp_r2i[:, ti * TAU: (ti + 1) * TAU], in_=rho2i)),
                   reads=("rho2i",), writes=("sp_r2i",))

        B.new_epoch()
        C2 = Carver()
        wres2 = C2.fp8(32 * 1024)
        k2t = C2.bf16(32 * TAU)
        r2it = C2.f32(TAU)
        evt_b2 = C2.f32(TAU)

        B.step("sync", "sLD",
               lambda e: fin(e, e.dma_start(out=wres2, in_=sp_w2q[:])),
               reads=("spwq",), writes=("wres",), extra=(B.ready.get("hbf"),))

        for ti in range(NT):
            c0 = ti * TAU
            B.step("sync", "sLD",
                   lambda e, ti=ti: fin(e, e.dma_start(
                       out=k2t, in_=sp_k2[:, ti * 32 * TAU: (ti + 1) * 32 * TAU])),
                   reads=("sp_k2",), writes=("k2t",))
            B.step("sync", "sLD",
                   lambda e, ti=ti: fin(e, e.dma_start(
                       out=r2it, in_=sp_r2i[:, ti * TAU: (ti + 1) * TAU])),
                   reads=("sp_r2i",), writes=("r2it",))

            for G in range(8):
                sl = G % 2

                def f_mm(e, G=G, sl=sl):
                    for k in range(32):
                        inst = e.matmul(ps[4 + sl][:, 0:TAU],
                                        wres2[:, k * 1024 + G * 128: k * 1024 + G * 128 + 128],
                                        k2t[:, _ts(k, TAU)], start=(k == 0), stop=(k == 31))
                    fin(e, inst)
                B.step("tensor", "sMM", f_mm, reads=("wres", "k2t"),
                       writes=(f"ps{4 + sl}",))

                xs = x_sb[:, G * T + c0: G * T + c0 + TAU]

                def f_u(e, G=G, sl=sl, xs=xs):
                    e.tensor_tensor(evt_b2, ps[4 + sl][:, 0:TAU], r2it, AL.mult)
                    e.tensor_scalar(evt_b2, evt_b2, mw_sb[:, 1:2], None, AL.mult)
                    e.tensor_scalar(evt_b2, evt_b2, b2L_sb[:, G:G + 1], 0.5, AL.add, AL.mult)
                    fin(e, e.tensor_tensor(xs, xs, evt_b2, AL.add))
                B.step("vector", "sV", f_u,
                       reads=(f"ps{4 + sl}", "r2it", "mw", "b2L", "evt_b2"),
                       writes=("x", "evt_b2"))


    def mark_mm_ready(*names):
        mmd = (B.sems["sMM"], B.sems["sMM"].n)
        for nm in names:
            B.ready[nm] = mmd
            B.readers[nm] = []

    def emit_xnb(SB, dst, scr, c0, tau, dres):
        for cc_ in range(NCH):
            xs = x_sb[:, cc_ * T + c0: cc_ * T + c0 + tau]

            def f_n(e, xs=xs, cc_=cc_):
                e.tensor_tensor(scr[:, 0:tau], xs, SB["m_t"][:, 0:tau], AL.subtract)
                fin(e, e.tensor_tensor(dst[:, cc_ * tau:(cc_ + 1) * tau], scr[:, 0:tau],
                                       SB["s_t"][:, 0:tau], AL.mult))
            B.step("vector", "sV", f_n, reads=("x", "m_t", "s_t", "scr"),
                   writes=(dres, "scr"))

    def emit_l1attn():
        TAU = 256
        NT = T // TAU

        def pmm(wr, xn, mc, tau, sl, wres_name, xname):
            def f_mm(e, mc=mc, sl=sl, tau=tau):
                for kc in range(8):
                    inst = e.matmul(ps[sl][:, 0:tau],
                                    wr[:, kc * 1024 + mc * 128: kc * 1024 + mc * 128 + 128],
                                    xn[:, _ts(kc, tau)], start=(kc == 0), stop=(kc == 7))
                fin(e, inst)
            B.step("tensor", "sMM", f_mm, reads=(wres_name, xname), writes=(f"ps{sl}",))
            mark_mm_ready(f"ps{sl}")

        B.new_epoch()
        C = Carver()
        wkr = C.bf16(8 * 1024)
        SB = mk_stats_bufs(C, TAU, want_r=False)
        kf = C.f32(8 * TAU)
        xnb = C.bf16(8 * TAU)
        eA = C.f32(TAU); eB = C.f32(TAU); eC = C.f32(TAU)
        scr = C.f32(TAU)
        ksa = C.f32(128)
        ksl = C.f32(8)

        B.step("gpsimd", "sGD",
               lambda e: fin(e, e.dma_start(
                   out=wkr.rearrange("p (c f) -> p c f", c=8),
                   in_=Wk.tensor.ap().rearrange("(c p) f -> p c f", p=128))),
               writes=("wres",))
        for ti in range(NT):
            c0 = ti * TAU
            emit_stats(SB, c0, TAU)
            emit_xnb(SB, xnb, scr, c0, TAU, "xnb")
            for mc in range(8):
                sl = mc % 2
                pmm(wkr, xnb, mc, TAU, sl, "wres", "xnb")
                B.step("vector", "sV",
                       lambda e, sl=sl: fin(e, e.tensor_scalar(
                           eA, ps[sl][:, 0:TAU], 0.0, None, AL.min)),
                       reads=(f"ps{sl}", "eA"), writes=("eA",))
                B.step("scalar", "sS",
                       lambda e: fin(e, e.activation(eB, eA, ACT.Exp)),
                       reads=("eA", "eB"), writes=("eB",))

                def f_e2(e, sl=sl, mc=mc, ti=ti):
                    e.tensor_scalar(eC, ps[sl][:, 0:TAU], 0.0, None, AL.max)
                    e.tensor_tensor(kf[:, _ts(mc, TAU)], eB, eC, AL.add)
                    fin(e, e.tensor_reduce(ksa[:, mc * 16 + ti: mc * 16 + ti + 1],
                                           kf[:, _ts(mc, TAU)], AX.X, AL.add))
                B.step("vector", "sV", f_e2, reads=(f"ps{sl}", "eB", "eC", "kf", "ksa"),
                       writes=("eC", "kf", "ksa"))

        def f_fin(e):
            e.memset(eA, 0.0)
            e.memset(eB, 0.0)
            fin(e, e.tensor_reduce(ksl[:, 0:8],
                                   ksa.rearrange("p (c t) -> p c t", c=8), AX.X, AL.add))
        B.step("vector", "sV", f_fin, reads=("ksa", "eA", "eB", "ksl"),
               writes=("ksl", "eA", "eB"))
        B.step("sync", "sST", lambda e: fin(e, e.dma_start(out=ar_in[:], in_=ksl[:, 0:8])),
               reads=("ksl",), writes=("arin",))
        B.step("gpsimd", "sG",
               lambda e: fin(e, e.collective_compute(
                   "AllReduce", AL.add, replica_groups=RG,
                   ins=[ar_in[:]], outs=[ar_out[:]])),
               reads=("arin",), writes=("arout",))
        B.step("sync", "sLD", lambda e: fin(e, e.dma_start(out=ks_sb[:], in_=ar_out[:])),
               reads=("arout",), writes=("ks",))

        B.new_epoch()
        C2 = Carver()
        wres5 = C2.bf16(8 * 1024)
        SB2 = mk_stats_bufs(C2, TAU, want_r=False)
        qb = C2.bf16(8 * TAU)
        kb = C2.bf16(8 * TAU)
        vf = C2.f32(8 * TAU)
        xnb2 = C2.bf16(8 * TAU)
        eA2 = C2.f32(TAU); eB2 = C2.f32(TAU); eC2 = C2.f32(TAU)
        scr2 = C2.f32(TAU)
        prodb = C2.bf16(TAU)
        rhob = C2.bf16(TAU)
        rf = C2.f32(TAU)

        def ld_w5(Wsrc):
            B.step("gpsimd", "sGD",
                   lambda e, Wsrc=Wsrc: fin(e, e.dma_start(
                       out=wres5.rearrange("p (c f) -> p c f", c=8),
                       in_=Wsrc.tensor.ap().rearrange("(c p) f -> p c f", p=128))),
                   writes=("wres",))

        def elu_bf(mc, sl, dst, dres):
            B.step("vector", "sV",
                   lambda e, sl=sl: fin(e, e.tensor_scalar(
                       eA2, ps[sl][:, 0:TAU], 0.0, None, AL.min)),
                   reads=(f"ps{sl}", "eA"), writes=("eA",))
            B.step("scalar", "sS",
                   lambda e: fin(e, e.activation(eB2, eA2, ACT.Exp)),
                   reads=("eA", "eB"), writes=("eB",))

            def f_e2(e, sl=sl, mc=mc):
                e.tensor_scalar(eC2, ps[sl][:, 0:TAU], 0.0, None, AL.max)
                fin(e, e.tensor_tensor(dst[:, _ts(mc, TAU)], eB2, eC2, AL.add))
            B.step("vector", "sV", f_e2, reads=(f"ps{sl}", "eB", "eC", dres),
                   writes=("eC", dres))

        for ti in range(NT):
            c0 = ti * TAU
            emit_stats(SB2, c0, TAU)
            emit_xnb(SB2, xnb2, scr2, c0, TAU, "xnb")
            ld_w5(Wq)
            for mc in range(8):
                pmm(wres5, xnb2, mc, TAU, mc % 2, "wres", "xnb")
                elu_bf(mc, mc % 2, qb, "qb")
            ld_w5(Wk)
            for mc in range(8):
                pmm(wres5, xnb2, mc, TAU, mc % 2, "wres", "xnb")
                elu_bf(mc, mc % 2, kb, "kb")
            for mc in range(8):
                B.step("vector", "sV",
                       lambda e, mc=mc: fin(e, e.tensor_tensor(
                           prodb, qb[:, _ts(mc, TAU)], kb[:, _ts(mc, TAU)], AL.mult)),
                       reads=("qb", "kb", "prodb"), writes=("prodb",))
                B.step("tensor", "sMM",
                       lambda e, mc=mc: fin(e, e.matmul(
                           ps[6][0:16, 0:TAU], ind_b[:, _ts(mc, 16)], prodb,
                           start=(mc == 0), stop=(mc == 7))),
                       reads=("prodb", "ind"), writes=("ps6",) if mc == 0 else ())
            for mc in range(8):
                B.step("vector", "sV",
                       lambda e, mc=mc: fin(e, e.tensor_scalar(
                           prodb, qb[:, _ts(mc, TAU)], ks_sb[:, mc:mc + 1], None, AL.mult)),
                       reads=("qb", "ks", "prodb"), writes=("prodb",))
                B.step("tensor", "sMM",
                       lambda e, mc=mc: fin(e, e.matmul(
                           ps[6][0:16, TAU:2 * TAU], ind_b[:, _ts(mc, 16)], prodb,
                           start=(mc == 0), stop=(mc == 7))),
                       reads=("prodb", "ind"), writes=())
            mark_mm_ready("ps6")
            B.step("scalar", "sS",
                   lambda e: fin(e, e.activation(eA2[0:16, :], ps[6][0:16, TAU:2 * TAU],
                                                 ACT.Ln, bias=ccol[0:16, 7:8])),
                   reads=("ps6", "eA"), writes=("eA",))
            B.step("scalar", "sS",
                   lambda e: fin(e, e.activation(eA2[0:16, :], eA2[0:16, :],
                                                 ACT.Exp, scale=-1.0)),
                   reads=("eA",), writes=("eA",))

            def f_rho2(e):
                e.tensor_tensor(rf[0:16, :], ps[6][0:16, 0:TAU], eA2[0:16, :], AL.mult)
                fin(e, e.tensor_copy(rhob[0:16, :], rf[0:16, :]))
            B.step("vector", "sV", f_rho2, reads=("ps6", "eA", "rf", "rhob"),
                   writes=("rf", "rhob"))

            ld_w5(Wv)
            for mc in range(8):
                sl = mc % 2
                pmm(wres5, xnb2, mc, TAU, sl, "wres", "xnb")
                B.step("vector", "sV",
                       lambda e, mc=mc, sl=sl: fin(e, e.tensor_copy(
                           vf[:, _ts(mc, TAU)], ps[sl][:, 0:TAU])),
                       reads=(f"ps{sl}", "vf"), writes=("vf",))
            for mc in range(8):
                B.step("tensor", "sMM",
                       lambda e, mc=mc: fin(e, e.matmul(
                           ps[7][:, 0:TAU], indtb[0:16, _ts(mc, 128)], rhob[0:16, 0:TAU],
                           start=True, stop=True)),
                       reads=("rhob", "indtb"), writes=("ps7",))
                mark_mm_ready("ps7")
                B.step("vector", "sV",
                       lambda e, mc=mc: fin(e, e.tensor_tensor(
                           vf[:, _ts(mc, TAU)], vf[:, _ts(mc, TAU)], ps[7][:, 0:TAU],
                           AL.mult)),
                       reads=("ps7", "vf"), writes=("vf",))
            ld_w5(Wg)
            for mc in range(8):
                sl = mc % 2
                pmm(wres5, xnb2, mc, TAU, sl, "wres", "xnb")
                B.step("vector", "sV",
                       lambda e, mc=mc, sl=sl: fin(e, e.tensor_copy(
                           kb[:, _ts(mc, TAU)], ps[sl][:, 0:TAU])),
                       reads=(f"ps{sl}", "kb"), writes=("kb",))
            emit_stats(SB2, 0, TAU, src=lambda cc_: vf[:, _ts(cc_, TAU)], srcres="vf")
            for mc in range(8):
                def f_zh(e, mc=mc):
                    zv = vf[:, _ts(mc, TAU)]
                    e.tensor_tensor(scr2, zv, SB2["m_t"][:, 0:TAU], AL.subtract)
                    e.tensor_tensor(scr2, scr2, SB2["s_t"][:, 0:TAU], AL.mult)
                    e.tensor_scalar(scr2, scr2, gam_sb[:, mc:mc + 1], None, AL.mult)
                    fin(e, e.tensor_scalar(qb[:, _ts(mc, TAU)], scr2,
                                           bet_sb[:, mc:mc + 1], None, AL.add))
                B.step("vector", "sV", f_zh,
                       reads=("vf", "m_t", "s_t", "gam", "bet", "scr", "qb"),
                       writes=("scr", "qb"))
                B.step("scalar", "sS",
                       lambda e, mc=mc: fin(e, e.activation(
                           eB2, kb[:, _ts(mc, TAU)], ACT.Sigmoid)),
                       reads=("kb", "eB"), writes=("eB",))

                def f_zg(e, mc=mc):
                    e.tensor_tensor(eC2, kb[:, _ts(mc, TAU)], eB2, AL.mult)
                    fin(e, e.tensor_tensor(xnb2[:, _ts(mc, TAU)], qb[:, _ts(mc, TAU)],
                                           eC2, AL.mult))
                B.step("vector", "sV", f_zg, reads=("kb", "eB", "eC", "qb", "xnb"),
                       writes=("eC", "xnb"))
            ld_w5(Wo)
            for mc in range(8):
                sl = mc % 2
                pmm(wres5, xnb2, mc, TAU, sl, "wres", "xnb")

                def f_out(e, mc=mc, sl=sl, c0=c0):
                    xs = x_sb[:, mc * T + c0: mc * T + c0 + TAU]
                    fin(e, e.tensor_tensor(xs, xs, ps[sl][:, 0:TAU], AL.add))
                B.step("vector", "sV", f_out, reads=(f"ps{sl}", "x"), writes=("x",))

    def emit_l0attn():
        CT = 128
        NCHK = T // CT

        B.new_epoch()
        C = Carver()
        wmv = C.bf16(2048)
        SB = mk_stats_bufs(C, CT, want_r=False)
        xnb = C.bf16(NCH * CT)
        ktok = C.bf16(1024)
        vone = C.bf16(16 * 65)
        eA = C.f32(256); eB = C.f32(256); eC = C.f32(256)
        scr = C.f32(CT)
        stt = C.f32(1040)
        brow_b = C.bf16(3072)[0:1, :]
        B.step("gpsimd", "sGD",
               lambda e: fin(e, e.dma_start(out=brow_b, in_=b_qkv_row_in[:])),
               writes=("brow",))

        def f_vo(e):
            v3 = vone.rearrange("p (h f) -> p h f", h=16)
            e.memset(v3[:, :, 64:65], 1.0)
            fin(e, e.memset(stt[0:64, :], 0.0))
        B.step("vector", "sV", f_vo, writes=("vone", "stt"))

        for c in range(NCHK):
            c0 = c * CT
            emit_stats(SB, c0, CT)
            emit_xnb(SB, xnb, scr, c0, CT, "xnb")
            for kc in range(8):
                B.step("gpsimd", "sGD",
                       lambda e, kc=kc: fin(e, e.dma_start(
                           out=wmv, in_=W_qkv.tensor.ap()[_ts(kc, 128), 1024:3072])),
                       writes=("wmv",))

                def f_mm(e, kc=kc):
                    for blk in range(4):
                        inst = e.matmul(ps[blk][:, 0:512], xnb[:, _ts(kc, CT)],
                                        wmv[:, _ts(blk, 512)],
                                        start=(kc == 0), stop=False)
                    fin(e, inst)
                B.step("tensor", "sMM", f_mm, reads=("wmv", "xnb"),
                       writes=("ps0", "ps1", "ps2", "ps3") if kc == 0 else ())

            def f_biasp(e):
                for blk in range(4):
                    inst = e.matmul(ps[blk][:, 0:512], ones_b[0:1, 0:128],
                                    brow_b[0:1, 1024 + blk * 512: 1024 + blk * 512 + 512],
                                    start=False, stop=True)
                fin(e, inst)
            B.step("tensor", "sMM", f_biasp, reads=("brow", "ones"), writes=())
            mark_mm_ready("ps0", "ps1", "ps2", "ps3")
            for blk in range(4):
                bank = blk // 2
                c0b = (blk % 2) * 256
                B.step("vector", "sV",
                       lambda e, bank=bank, c0b=c0b: fin(e, e.tensor_scalar(
                           eA, ps[bank][:, c0b:c0b + 256], 0.0, None, AL.min)),
                       reads=(f"ps{bank}", "eA"), writes=("eA",))
                B.step("scalar", "sS",
                       lambda e: fin(e, e.activation(eB, eA, ACT.Exp)),
                       reads=("eA", "eB"), writes=("eB",))

                def f_k2(e, bank=bank, c0b=c0b, blk=blk):
                    e.tensor_scalar(eC, ps[bank][:, c0b:c0b + 256], 0.0, None, AL.max)
                    fin(e, e.scalar_tensor_tensor(ktok[:, blk * 256:(blk + 1) * 256],
                                                  eB, 1e-4, eC, AL.add, AL.add))
                B.step("vector", "sV", f_k2, reads=(f"ps{bank}", "eB", "eC", "ktok"),
                       writes=("eC", "ktok"))
            for h in range(16):
                bank = 2 + h // 8
                cb = (h % 8) * 64
                B.step("vector", "sV",
                       lambda e, bank=bank, cb=cb, h=h: fin(e, e.tensor_copy(
                           vone[:, h * 65: h * 65 + 64], ps[bank][:, cb:cb + 64])),
                       reads=(f"ps{bank}", "vone"), writes=("vone",))
            for h in range(16):
                bank = 4 + h % 2
                B.step("tensor", "sMM",
                       lambda e, h=h, bank=bank: fin(e, e.matmul(
                           ps[bank][0:64, 0:65], ktok[:, _ts(h, 64)],
                           vone[:, h * 65:(h + 1) * 65],
                           start=True, stop=True)),
                       reads=("ktok", "vone"), writes=(f"ps{bank}",))
                mark_mm_ready(f"ps{bank}")
                B.step("vector", "sV",
                       lambda e, h=h, bank=bank: fin(e, e.tensor_tensor(
                           stt[0:64, h * 65:(h + 1) * 65],
                           stt[0:64, h * 65:(h + 1) * 65],
                           ps[bank][0:64, 0:65], AL.add)),
                       reads=(f"ps{bank}", "stt"), writes=("stt",))

        def f_sev(e):
            fin(e, e.tensor_scalar(stt[0:64, :], stt[0:64, :], fli_sb[0:64, 0:1],
                                   None, AL.mult))
        B.step("vector", "sV", f_sev, reads=("fli", "stt"), writes=("stt",))
        B.step("sync", "sST", lambda e: fin(e, e.dma_start(out=cc_in[:], in_=stt[0:64, :])),
               reads=("stt",), writes=("ccin",))
        B.step("gpsimd", "sG",
               lambda e: fin(e, e.collective_compute(
                   "AllReduce", AL.add, replica_groups=RG,
                   ins=[cc_in[:]], outs=[cc_out[:]])),
               reads=("ccin",), writes=("ccout",))

        B.new_epoch()
        C2 = Carver()
        wmv2 = C2.bf16(3072)
        SB2 = mk_stats_bufs(C2, CT, want_r=False)
        xnb2 = C2.bf16(NCH * CT)
        qtf = C2.f32(1024)
        ktf = C2.f32(1024)
        ktok2 = C2.bf16(1024)
        vone2 = C2.bf16(16 * 65)
        attnf = C2.f32(1024)
        afm = C2.bf16(1024)
        state = C2.f32(1040)
        stateb = C2.bf16(1040)
        qfm = C2.bf16(128)
        kfm = C2.bf16(128)
        stm = C2.bf16(128)
        eA2 = C2.f32(256); eB2 = C2.f32(256); eC2 = C2.f32(256)
        scr2 = C2.f32(CT)
        rcol = C2.f32(1)
        evt5 = C2.f32(256)
        wob = C2.bf16(1024)
        oacc = C2.f32(1024)
        brow_b = C2.bf16(3072)[0:1, :]
        B.step("gpsimd", "sGD",
               lambda e: fin(e, e.dma_start(out=brow_b, in_=b_qkv_row_in[:])),
               writes=("brow",))

        B.step("sync", "sLD",
               lambda e: fin(e, e.dma_start(out=state[0:64, :], in_=cc_out[:])),
               reads=("ccout",), writes=("state",))

        def f_si(e):
            e.tensor_scalar(state[0:64, :], state[0:64, :], flag_sb[0:64, 0:1],
                            None, AL.mult)
            e.tensor_copy(stateb[0:64, :], state[0:64, :])
            v3 = vone2.rearrange("p (h f) -> p h f", h=16)
            fin(e, e.memset(v3[:, :, 64:65], 1.0))
        B.step("vector", "sV", f_si,
               reads=("state", "flag", "stateb", "vone"),
               writes=("state", "stateb", "vone"))

        for c in range(NCHK):
            c0 = c * CT
            emit_stats(SB2, c0, CT)
            emit_xnb(SB2, xnb2, scr2, c0, CT, "xnb")
            for kc in range(8):
                B.step("gpsimd", "sGD",
                       lambda e, kc=kc: fin(e, e.dma_start(
                           out=wmv2, in_=W_qkv.tensor.ap()[_ts(kc, 128), :])),
                       writes=("wmv",))

                def f_mm(e, kc=kc):
                    for blk in range(6):
                        inst = e.matmul(ps[blk][:, 0:512], xnb2[:, _ts(kc, CT)],
                                        wmv2[:, _ts(blk, 512)],
                                        start=(kc == 0), stop=False)
                    fin(e, inst)
                B.step("tensor", "sMM", f_mm, reads=("wmv", "xnb"),
                       writes=tuple(f"ps{i}" for i in range(6)) if kc == 0 else ())

            def f_bias(e):
                for blk in range(6):
                    inst = e.matmul(ps[blk][:, 0:512], ones_b[0:1, 0:128],
                                    brow_b[0:1, _ts(blk, 512)], start=False, stop=True)
                fin(e, inst)
            B.step("tensor", "sMM", f_bias, reads=("brow", "ones"), writes=())
            mark_mm_ready(*[f"ps{i}" for i in range(6)])

            for blk in range(4):
                bank = blk // 2
                c0b = (blk % 2) * 256
                B.step("vector", "sV",
                       lambda e, bank=bank, c0b=c0b: fin(e, e.tensor_scalar(
                           eA2, ps[bank][:, c0b:c0b + 256], 0.125, 0.0, AL.mult, AL.min)),
                       reads=(f"ps{bank}", "eA"), writes=("eA",))
                B.step("scalar", "sS",
                       lambda e: fin(e, e.activation(eB2, eA2, ACT.Exp)),
                       reads=("eA", "eB"), writes=("eB",))

                def f_q2(e, bank=bank, c0b=c0b, blk=blk):
                    e.tensor_scalar(eC2, ps[bank][:, c0b:c0b + 256], 0.125, 0.0,
                                    AL.mult, AL.max)
                    fin(e, e.scalar_tensor_tensor(qtf[:, blk * 256:(blk + 1) * 256],
                                                  eB2, 1e-4, eC2, AL.add, AL.add))
                B.step("vector", "sV", f_q2, reads=(f"ps{bank}", "eB", "eC", "qtok"),
                       writes=("eC", "qtok"))
            for blk in range(4):
                bank = 2 + blk // 2
                c0b = (blk % 2) * 256
                B.step("vector", "sV",
                       lambda e, bank=bank, c0b=c0b: fin(e, e.tensor_scalar(
                           eA2, ps[bank][:, c0b:c0b + 256], 0.0, None, AL.min)),
                       reads=(f"ps{bank}", "eA"), writes=("eA",))
                B.step("scalar", "sS",
                       lambda e: fin(e, e.activation(eB2, eA2, ACT.Exp)),
                       reads=("eA", "eB"), writes=("eB",))

                def f_k2(e, bank=bank, c0b=c0b, blk=blk):
                    e.tensor_scalar(eC2, ps[bank][:, c0b:c0b + 256], 0.0, None, AL.max)
                    e.scalar_tensor_tensor(ktf[:, blk * 256:(blk + 1) * 256],
                                           eB2, 1e-4, eC2, AL.add, AL.add)
                    fin(e, e.tensor_copy(ktok2[:, blk * 256:(blk + 1) * 256],
                                         ktf[:, blk * 256:(blk + 1) * 256]))
                B.step("vector", "sV", f_k2, reads=(f"ps{bank}", "eB", "eC", "ktok"),
                       writes=("eC", "ktok"))
            for h in range(16):
                bank = 4 + h // 8
                cb = (h % 8) * 64
                B.step("vector", "sV",
                       lambda e, bank=bank, cb=cb, h=h: fin(e, e.tensor_copy(
                           vone2[:, h * 65: h * 65 + 64], ps[bank][:, cb:cb + 64])),
                       reads=(f"ps{bank}", "vone"), writes=("vone",))

            for h in range(16):
                B.step("tensor", "sMM",
                       lambda e, h=h: fin(e, e.transpose(
                           ps[6][0:64, 0:128], qtf[:, _ts(h, 64)], idf_sb[:])),
                       reads=("qtok", "ident"), writes=("ps6",))
                B.step("vector", "sV",
                       lambda e: fin(e, e.tensor_copy(qfm[0:64, :], ps[6][0:64, 0:128])),
                       reads=("ps6", "qfm"), writes=("qfm",))
                B.step("tensor", "sMM",
                       lambda e, h=h: fin(e, e.transpose(
                           ps[7][0:64, 0:128], ktf[:, _ts(h, 64)], idf_sb[:])),
                       reads=("ktok", "ident"), writes=("ps7",))
                B.step("vector", "sV",
                       lambda e: fin(e, e.tensor_copy(kfm[0:64, :], ps[7][0:64, 0:128])),
                       reads=("ps7", "kfm"), writes=("kfm",))
                B.step("tensor", "sMM",
                       lambda e: fin(e, e.matmul(ps[6][:, 0:128], kfm[0:64, 0:128],
                                                 qfm[0:64, 0:128], start=True, stop=True)),
                       reads=("kfm", "qfm"), writes=("ps6",))
                mark_mm_ready("ps6")
                B.step("vector", "sV",
                       lambda e: fin(e, e.tensor_tensor(
                           stm[:, 0:128], ps[6][:, 0:128], tri_sb[:], AL.mult)),
                       reads=("ps6", "tri", "stm"), writes=("stm",))

                def f_nd(e, h=h):
                    e.matmul(ps[7][:, 0:65], stm[:, 0:128],
                             vone2[:, h * 65:(h + 1) * 65], start=True, stop=False)
                    fin(e, e.matmul(ps[7][:, 0:65], qfm[0:64, 0:128],
                                    stateb[0:64, h * 65:(h + 1) * 65],
                                    start=False, stop=True))
                B.step("tensor", "sMM", f_nd,
                       reads=("stm", "vone", "qfm", "stateb"), writes=("ps7",))
                mark_mm_ready("ps7")
                B.step("scalar", "sS",
                       lambda e: fin(e, e.activation(rcol[:, 0:1], ps[7][:, 64:65],
                                                     ACT.Ln, bias=ccol[:, 7:8])),
                       reads=("ps7", "rcol"), writes=("rcol",))
                B.step("scalar", "sS",
                       lambda e: fin(e, e.activation(rcol[:, 0:1], rcol[:, 0:1],
                                                     ACT.Exp, scale=-1.0)),
                       reads=("rcol",), writes=("rcol",))
                B.step("vector", "sV",
                       lambda e, h=h: fin(e, e.tensor_scalar(
                           attnf[:, _ts(h, 64)], ps[7][:, 0:64], rcol[:, 0:1],
                           None, AL.mult)),
                       reads=("ps7", "rcol", "attn"), writes=("attn",))
                B.step("tensor", "sMM",
                       lambda e, h=h: fin(e, e.matmul(
                           ps[4][0:64, 0:65], ktok2[:, _ts(h, 64)],
                           vone2[:, h * 65:(h + 1) * 65], start=True, stop=True)),
                       reads=("ktok", "vone"), writes=("ps4",))
                mark_mm_ready("ps4")
                B.step("vector", "sV",
                       lambda e, h=h: fin(e, e.tensor_tensor(
                           state[0:64, h * 65:(h + 1) * 65],
                           state[0:64, h * 65:(h + 1) * 65],
                           ps[4][0:64, 0:65], AL.add)),
                       reads=("ps4", "state"), writes=("state",))
            B.step("vector", "sV",
                   lambda e: fin(e, e.tensor_copy(stateb[0:64, :], state[0:64, :])),
                   reads=("state", "stateb"), writes=("stateb",))

            for fc in range(8):
                B.step("tensor", "sMM",
                       lambda e, fc=fc: fin(e, e.transpose(
                           ps[3][:, 0:128], attnf[:, _ts(fc, 128)], idf_sb[:])),
                       reads=("attn", "ident"), writes=("ps3",))
                mark_mm_ready("ps3")
                B.step("vector", "sV",
                       lambda e, fc=fc: fin(e, e.tensor_copy(
                           afm[:, _ts(fc, 128)], ps[3][:, 0:128])),
                       reads=("ps3", "afm"), writes=("afm",))
            for fc in range(8):
                B.step("gpsimd", "sGD",
                       lambda e, fc=fc: fin(e, e.dma_start(
                           out=wob, in_=W_out.tensor.ap()[_ts(fc, 128), :])),
                       writes=("wob",))

                def f_wm(e, fc=fc):
                    for mc in range(8):
                        inst = e.matmul(
                            ps[mc // 4][:, (mc % 4) * 128:(mc % 4) * 128 + 128],
                            wob[:, _ts(mc, 128)], afm[:, _ts(fc, 128)],
                            start=True, stop=True)
                    fin(e, inst)
                B.step("tensor", "sMM", f_wm, reads=("wob", "afm"),
                       writes=("ps0", "ps1"))
                mark_mm_ready("ps0", "ps1")

                def f_oa(e, fc=fc):
                    for mc in range(8):
                        reg = ps[mc // 4][:, (mc % 4) * 128:(mc % 4) * 128 + 128]
                        if fc == 0:
                            inst = e.tensor_copy(oacc[:, _ts(mc, 128)], reg)
                        else:
                            inst = e.tensor_tensor(oacc[:, _ts(mc, 128)],
                                                   oacc[:, _ts(mc, 128)], reg, AL.add)
                    fin(e, inst)
                B.step("vector", "sV", f_oa, reads=("ps0", "ps1", "oacc"),
                       writes=("oacc",))
            for mc in range(8):
                def f_oe(e, mc=mc, gc0=c * CT):
                    xs = x_sb[:, mc * T + gc0: mc * T + gc0 + 128]
                    e.tensor_scalar(evt5[:, 0:128], oacc[:, _ts(mc, 128)],
                                    bout_sb[:, mc:mc + 1], None, AL.add)
                    fin(e, e.tensor_tensor(xs, xs, evt5[:, 0:128], AL.add))
                B.step("vector", "sV", f_oe,
                       reads=("oacc", "bout", "evt", "x"),
                       writes=("x", "evt"))

    if 4 in sublayers or 6 in sublayers or 103 in sublayers:
        Cw = Carver()
        emit_wstats(Cw)
        emit_wquant()
    def d_dbg_f32():
        return dbg_f[:]
    if 103 in sublayers:
        B.new_epoch()
        C3 = Carver()
        wq_r = C3.fp8(4096)
        B.step("sync", "sLD",
               lambda e: fin(e, e.dma_start(out=wq_r[:, 0:4096], in_=sp_w1q[:, 0:4096])),
               reads=("spwq",), writes=("wqr",), extra=B.epoch)
        B.step("vector", "sV",
               lambda e: fin(e, e.tensor_copy(d_dbg_f32()[:, 0:4096], wq_r[:, 0:4096])),
               reads=("wqr",), writes=("dbg1",))
        B.step("sync", "sST",
               lambda e: fin(e, e.dma_start(out=dbg_d[:, 0:4096], in_=d_dbg_f32()[:, 0:4096])),
               reads=("dbg1",), writes=("dbgo",))
        B.step("sync", "sLD",
               lambda e: fin(e, e.dma_start(out=wq_r[:, 0:1024], in_=sp_w2q[:, 0:1024])),
               reads=("wqr",), writes=("wqr",))
        B.step("vector", "sV",
               lambda e: fin(e, e.tensor_copy(d_dbg_f32()[:, 0:1024], wq_r[:, 0:1024])),
               reads=("wqr",), writes=("dbg2",))
        def f_d3(e):
            sh = B.cur[0]
            e.dma_start(out=dbg_d[:, 4096:5120], in_=d_dbg_f32()[:, 0:1024]).then_inc(sh, 16)
            e.dma_start(out=dbg_d[:, 5120:5122], in_=mw_sb[:]).then_inc(sh, 16)
            e.dma_start(out=dbg_d[:, 5122:5124], in_=sw_sb[:]).then_inc(sh, 16)
            fin(e, e.dma_start(out=dbg_d[:, 5124:5136], in_=wsa[:]))
        B.step("sync", "sST", f_d3, reads=("dbg2", "mw", "sw"), writes=("dbgo",), n_dma=4)
    if 102 in sublayers:
        emit_l0ffn("dbg", phases=(1,))
    if 101 in sublayers:
        emit_l0ffn("s1", phases=(1,))
        B.new_epoch()
        B.step("gpsimd", "sGD",
               lambda e: fin(e, e.dma_start(out=out_d[:], in_=sp_h8[:, 0:NCH * T])),
               reads=("sp_h8",), writes=("out",), extra=B.epoch)
    if 1 in sublayers:
        emit_l0ffn("s1")
    if 2 in sublayers:
        emit_l0attn()
    if 3 in sublayers:
        emit_l0ffn("s3")
    if 4 in sublayers:
        emit_l1ffn("s4")
    if 5 in sublayers:
        emit_l1attn()
    if 6 in sublayers:
        emit_l1ffn("s6")

    def f_store2(e):
        inst = e.dma_start(out=out_d[:], in_=x_sb[:])
        fin(e, inst)
    st_sem, st_final = B.step("sync", "sST", f_store2, reads=("x",), writes=("out",))

    def f_wait(e):
        e.wait_ge(st_sem.h, st_final)
    B.items.append(("sync", [], f_wait, st_sem))

    with nc.Block() as blk:
        @blk.sync
        def _(e):
            B.replay("sync", e)

        @blk.vector
        def _(e):
            B.replay("vector", e)

        @blk.scalar
        def _(e):
            B.replay("scalar", e)

        @blk.tensor
        def _(e):
            B.replay("tensor", e)

        @blk.gpsimd
        def _(e):
            B.replay("gpsimd", e)

    st.close()
    return nc


def _prep_inputs(inputs):
    f32 = np.float32
    x = np.asarray(inputs["L0_attn_x"], f32)
    keep = _dropout_keep_mask(42, (4, 8192, 1024), 1.0 - 0.2)
    maskv = (keep.astype(f32) * f32(0.625)).astype(BFnp)

    tri = np.triu(np.ones((128, 128), f32)).astype(BFnp)
    ind = np.zeros((128, NCH * 16), f32)
    indT = np.zeros((16, NCH * 128), f32)
    for c in range(NCH):
        for p in range(128):
            h = (c * 128 + p) // 64
            ind[p, c * 16 + h] = 1.0
            indT[h, c * 128 + p] = 1.0

    def fm(v, k):
        return np.ascontiguousarray(np.asarray(v, f32).reshape(k, 128).T)

    shared = {
        "L0_attn_W_qkv": np.ascontiguousarray(inputs["L0_attn_W_qkv"], f32),
        "b_qkv_fm": fm(inputs["L0_attn_b_qkv"], 24),
        "b_qkv_row": np.ascontiguousarray(np.asarray(inputs["L0_attn_b_qkv"], f32).reshape(1, 3072)),
        "L0_attn_W_out": np.ascontiguousarray(inputs["L0_attn_W_out"], f32),
        "b_out_fm": fm(inputs["L0_attn_b_out"], 8),
        "W1_T": np.ascontiguousarray(np.asarray(inputs["L0_ffn_W1"], f32).T),
        "b1_fm": fm(inputs["L0_ffn_b1"], 32),
        "W2_T": np.ascontiguousarray(np.asarray(inputs["L0_ffn_W2"], f32).T),
        "b2_fm": fm(inputs["L0_ffn_b2"], 8),
        "L1_attn_Wq": np.ascontiguousarray(inputs["L1_attn_Wq"], f32),
        "L1_attn_Wk": np.ascontiguousarray(inputs["L1_attn_Wk"], f32),
        "L1_attn_Wv": np.ascontiguousarray(inputs["L1_attn_Wv"], f32),
        "L1_attn_Wg": np.ascontiguousarray(inputs["L1_attn_Wg"], f32),
        "L1_attn_Wo": np.ascontiguousarray(inputs["L1_attn_Wo"], f32),
        "gamma_fm": fm(inputs["L1_attn_gamma"], 8),
        "beta_fm": fm(inputs["L1_attn_beta"], 8),
        "w1L_T": np.ascontiguousarray(np.asarray(inputs["L1_ffn_w1"], f32).T),
        "b1L_fm": fm(inputs["L1_ffn_b1"], 32),
        "w2L_T": np.ascontiguousarray(np.asarray(inputs["L1_ffn_w2"], f32).T),
        "b2L_fm": fm(inputs["L1_ffn_b2"], 8),
        "tri": tri,
        "ident_f": np.eye(128, dtype=f32),
        "ind": ind,
        "indT": indT,
    }

    in_maps = []
    for core in range(8):
        b, half = core // 2, core % 2
        sl = slice(half * 4096, (half + 1) * 4096)
        xs = x[b, sl].T.reshape(NCH, 128, 4096).transpose(1, 0, 2).reshape(128, NCH * 4096)
        ms = maskv[b, sl].T.reshape(NCH, 128, 4096).transpose(1, 0, 2).reshape(128, NCH * 4096)
        m = dict(shared)
        m["x_fm"] = np.ascontiguousarray(xs)
        m["mask_fm"] = np.ascontiguousarray(ms)
        m["flag"] = np.full((128, 1), float(half), f32)
        in_maps.append(m)
    return in_maps


def kernel(**inputs):
    in_maps = _prep_inputs(inputs)
    nc = build_program()
    res = run_bass_kernel_spmd(nc, in_maps, core_ids=list(range(8)))
    out = np.empty((4, 8192, 1024), np.float32)
    for core in range(8):
        b, half = core // 2, core % 2
        o = np.asarray(res.results[core]["out_fm"])
        o = o.reshape(128, NCH, 4096).transpose(1, 0, 2).reshape(1024, 4096)
        out[b, half * 4096:(half + 1) * 4096] = o.T
    return out


# version 04
# speedup vs baseline: 1.0009x; 1.0009x over previous
import contextlib
import numpy as np
import ml_dtypes
import concourse.bass as bass
import concourse.mybir as mybir
from concourse.bass_utils import run_bass_kernel_spmd

F32 = mybir.dt.float32
BF16 = mybir.dt.bfloat16
FP8 = mybir.dt.float8e5
F32R = mybir.dt.float32r
AL = mybir.AluOpType
ACT = mybir.ActivationFunctionType
AX = mybir.AxisListType

BFnp = ml_dtypes.bfloat16

MAGIC = float(np.float32(1.5 * 2.0 ** 23))
LN1024 = float(np.log(np.float64(1024.0)))
LN127 = float(np.log(np.float64(127.0)))
B_N2EPS = float(np.float32(1024.0 * 1024.0 * 1e-5))

D = 1024
T = 4096
NCH = 8

SUBLAYERS = (1, 2, 3, 4, 5, 6)


def _threefry2x32(k0, k1, x0, x1):
    ks0 = np.uint32(k0); ks1 = np.uint32(k1)
    ks2 = np.uint32(np.uint32(0x1BD11BDA) ^ ks0 ^ ks1)
    x0 = (x0 + ks0).astype(np.uint32); x1 = (x1 + ks1).astype(np.uint32)
    R = [(13, 15, 26, 6), (17, 29, 16, 24)]
    ks = (ks0, ks1, ks2)
    for i in range(5):
        r = R[i % 2]
        for j in range(4):
            x0 = (x0 + x1).astype(np.uint32)
            x1 = ((x1 << np.uint32(r[j])) | (x1 >> np.uint32(32 - r[j]))).astype(np.uint32)
            x1 = x1 ^ x0
        x0 = (x0 + ks[(i + 1) % 3]).astype(np.uint32)
        x1 = (x1 + ks[(i + 2) % 3] + np.uint32(i + 1)).astype(np.uint32)
    return x0, x1


def _dropout_keep_mask(seed, shape, p_keep):
    n = int(np.prod(shape))
    idx = np.arange(n, dtype=np.uint64)
    c1 = (idx >> np.uint64(32)).astype(np.uint32)
    c2 = (idx & np.uint64(0xFFFFFFFF)).astype(np.uint32)
    b1, b2 = _threefry2x32(0, seed, c1, c2)
    bits = b1 ^ b2
    f = ((bits >> np.uint32(9)) | np.uint32(0x3F800000)).view(np.float32) - np.float32(1.0)
    u = np.maximum(f, np.float32(0.0))
    return (u < np.float32(p_keep)).reshape(shape)


def _ts(i, n):
    return slice(i * n, (i + 1) * n)


class Sem:
    def __init__(self, handle, dma):
        self.h = handle
        self.n = 0
        self.step_v = 16 if dma else 1

    def inc(self):
        self.n += self.step_v
        return self.n


class Builder:

    DMA_SEMS = ("sLD", "sGD", "sXP", "sST")

    def __init__(self, nc, stack):
        self.nc = nc
        self.stack = stack
        self.items = []
        self.ready = {}
        self.readers = {}
        self.epoch = ()
        self.sems = {}
        self._semct = 0
        for name, dma in (("sMM", 0), ("sV", 0), ("sS", 0), ("sG", 0)):
            h = stack.enter_context(nc.semaphore(name))
            self.sems[name] = Sem(h, dma)

    def _dma_sem(self, key):
        if key not in self.sems:
            self._semct += 1
            h = self.stack.enter_context(self.nc.semaphore(f"dq{self._semct}"))
            self.sems[key] = Sem(h, 1)
        return self.sems[key]

    def step(self, engine, sem_name, fn, reads=(), writes=(), extra=(), n_dma=1):
        if sem_name in self.DMA_SEMS:
            key = sem_name + ":" + (writes[0] if writes else
                                    (reads[0] if reads else "anon"))
            sem = self._dma_sem(key)
        else:
            sem = self.sems[sem_name]
        waits = {}
        def add(mk):
            if mk is None:
                return
            s, v = mk
            if waits.get(id(s), (None, -1))[1] < v:
                waits[id(s)] = (s, v)
        for r in reads:
            add(self.ready.get(r))
        for w in writes:
            add(self.ready.get(w))
            for mk in self.readers.get(w, ()):
                add(mk)
        for mk in extra:
            add(mk)
        if writes:
            for mk in self.epoch:
                add(mk)
        mark = sem.inc()
        for _ in range(n_dma - 1):
            mark = sem.inc()
        wl = list(waits.values())
        self.items.append((engine, wl, fn, sem))
        mk = (sem, mark)
        for w in writes:
            self.ready[w] = mk
            self.readers[w] = []
        for r in reads:
            self.readers.setdefault(r, []).append(mk)
        return mk

    def new_epoch(self):
        self.epoch = tuple((s, s.n) for s in self.sems.values() if s.n > 0)

    def replay(self, engine, eng):
        for e, wl, fn, sem in self.items:
            if e != engine:
                continue
            for s, v in wl:
                eng.wait_ge(s.h, v)
            self.cur = (sem.h, sem.step_v)
            fn(eng)

    def fin(self, eng):
        sh, sv = self.cur
        return lambda inst: inst.then_inc(sh, sv)


def build_program(sublayers=SUBLAYERS):
    nc = bass.Bass("TRN2", target_bir_lowering=False, debug=False,
                   detect_race_conditions=False)
    st = contextlib.ExitStack()

    def din(name, shape, dt=F32):
        return nc.dram_tensor(name, list(shape), dt, kind="ExternalInput").ap()

    x_in = din("x_fm", [128, NCH * T])
    mask_in = din("mask_fm", [128, NCH * T], BF16)
    flag_in = din("flag", [128, 1])
    tri_in = din("tri", [128, 128], BF16)
    idf_in = din("ident_f", [128, 128])
    ind_in = din("ind", [128, NCH * 16])
    indT_in = din("indT", [16, NCH * 128])
    W_qkv = din("L0_attn_W_qkv", [1024, 3072])
    b_qkv = din("b_qkv_fm", [128, 24])
    b_qkv_row_in = din("b_qkv_row", [1, 3072])
    W_out = din("L0_attn_W_out", [1024, 1024])
    b_out = din("b_out_fm", [128, 8])
    W1 = din("W1_T", [1024, 4096])
    b1 = din("b1_fm", [128, 32])
    W2 = din("W2_T", [4096, 1024])
    b2 = din("b2_fm", [128, 8])
    Wq = din("L1_attn_Wq", [1024, 1024])
    Wk = din("L1_attn_Wk", [1024, 1024])
    Wv = din("L1_attn_Wv", [1024, 1024])
    Wg = din("L1_attn_Wg", [1024, 1024])
    Wo = din("L1_attn_Wo", [1024, 1024])
    gam = din("gamma_fm", [128, 8])
    bet = din("beta_fm", [128, 8])
    w1L = din("w1L_T", [1024, 4096])
    b1L = din("b1L_fm", [128, 32])
    w2L = din("w2L_T", [4096, 1024])
    b2L = din("b2L_fm", [128, 8])
    out_d = nc.dram_tensor("out_fm", [128, NCH * T], F32, kind="ExternalOutput").ap()
    dbg_d = (nc.dram_tensor("dbg_fm", [128, 8192], F32, kind="ExternalOutput").ap()
             if any(s > 100 for s in sublayers) else None)

    NW, WTOK = 16, 256
    sp_q = nc.dram_tensor("sp_q", [128, NCH * T], BF16, kind="Internal").ap()
    sp_k = nc.dram_tensor("sp_k", [128, NCH * T], BF16, kind="Internal").ap()
    sp_v = nc.dram_tensor("sp_v", [128, NW * 2 * 1040], BF16, kind="Internal").ap()
    sp_kt = nc.dram_tensor("sp_kt", [128, NCH * T], BF16, kind="Internal").ap()
    cc_in = nc.dram_tensor("cc_in", [64, 1040], F32, kind="Internal").ap()
    cc_out = nc.dram_tensor("cc_out", [64, 1040], F32, kind="Internal").ap()
    ar_in = nc.dram_tensor("ar_in", [128, 8], F32, kind="Internal").ap()
    ar_out = nc.dram_tensor("ar_out", [128, 8], F32, kind="Internal").ap()
    sp_h8 = nc.dram_tensor("sp_h8", [128, 32 * T], FP8, kind="Internal").ap()
    sp_k2 = nc.dram_tensor("sp_k2", [128, 32 * T], BF16, kind="Internal").ap()
    sp_r2i = nc.dram_tensor("sp_r2i", [128, T], F32, kind="Internal").ap()
    sp_w1q = nc.dram_tensor("sp_w1q", [128, 8 * 4096], FP8, kind="Internal").ap()
    sp_w2q = nc.dram_tensor("sp_w2q", [128, 32 * 1024], FP8, kind="Internal").ap()
    RG = [[0, 1], [2, 3], [4, 5], [6, 7]]

    def sbuf(name, shape, dt):
        return st.enter_context(nc.sbuf_tensor(name, list(shape), dt))

    x_sb = sbuf("x_sb", [128, NCH * T], F32)
    ones_b = sbuf("ones_b", [128, 128], BF16)
    ones_f = sbuf("ones_f", [1, 128], F32)
    tri_sb = sbuf("tri_sb", [128, 128], BF16)
    idf_sb = sbuf("idf_sb", [128, 128], F32)
    ind_b = sbuf("ind_b", [128, NCH * 16], BF16)
    flag_sb = sbuf("flag_sb", [128, 1], F32)
    bqkv_sb = sbuf("bqkv_sb", [128, 24], F32)
    bout_sb = sbuf("bout_sb", [128, 8], F32)
    b1q_sb = sbuf("b1q_sb", [128, 32], F32)
    b2q_sb = sbuf("b2q_sb", [128, 8], F32)
    b1L_sb = sbuf("b1L_sb", [128, 32], F32)
    b2L_sb = sbuf("b2L_sb", [128, 8], F32)
    gam_sb = sbuf("gam_sb", [128, 8], F32)
    bet_sb = sbuf("bet_sb", [128, 8], F32)
    t8a = sbuf("t8a", [128, 32], FP8)
    mw_sb = sbuf("mw_sb", [128, 2], F32)
    ccol = sbuf("ccol", [128, 8], F32)
    pm_col = sbuf("pm_col", [128, 4], F32)
    pm_row = sbuf("pm_row", [1, 512], F32)
    xb_g = [sbuf("xb_g0", [128, 512], BF16), sbuf("xb_g1", [128, 512], BF16)]
    sqb_g = [sbuf("sqb_g0", [128, 512], BF16), sbuf("sqb_g1", [128, 512], BF16)]
    sw_sb = sbuf("sw_sb", [128, 2], F32)
    wsa = sbuf("wsa", [128, 12], F32)
    ident_b = sbuf("ident_b", [128, 128], BF16)
    fli_sb = sbuf("fli_sb", [128, 1], F32)
    indtb = sbuf("indtb", [16, NCH * 128], BF16)
    ks_sb = sbuf("ks_sb", [128, 8], F32)

    dbg_f = sbuf("dbgf", [128, NCH * 512], F32) if any(s > 100 for s in sublayers) else None
    AR_COLS = (nc.sbuf_bytes_remaining - 64) // 4 // 8 * 8
    arena = sbuf("arena", [128, AR_COLS], F32)
    stage = arena[:, 0:128]

    class Carver:
        def __init__(self):
            self.off = 0

        def take(self, nbytes):
            o = self.off
            assert o + nbytes <= AR_COLS * 4, (o, nbytes)
            self.off += (nbytes + 31) // 32 * 32
            return o

        def f32(self, cols):
            o = self.take(cols * 4)
            return arena[:, o // 4: o // 4 + cols]

        def bf16(self, cols):
            o = self.take(cols * 2)
            return arena[:].bitcast(BF16)[:, o // 2: o // 2 + cols]

        def fp8(self, cols):
            o = self.take(cols)
            return arena[:].bitcast(FP8)[:, o: o + cols]

        def f32r(self, cols):
            o = self.take(cols * 4)
            return arena[:].bitcast(F32R)[:, o // 4: o // 4 + cols]

    ps = [st.enter_context(nc.psum_tensor(f"ps{i}", [128, 512], F32)) for i in range(8)]

    B = Builder(nc, st)
    FIN = B.fin

    def fin(e, inst):
        sh, sv = B.cur
        inst.then_inc(sh, sv)

    def pre_sync(e):
        sh = B.cur[0]
        e.dma_start(out=x_sb[:], in_=x_in[:]).then_inc(sh, 16)
        e.dma_start(out=tri_sb[:], in_=tri_in[:]).then_inc(sh, 16)
        e.dma_start(out=idf_sb[:], in_=idf_in[:]).then_inc(sh, 16)
        e.dma_start(out=flag_sb[:], in_=flag_in[:]).then_inc(sh, 16)
        e.dma_start(out=bqkv_sb[:], in_=b_qkv[:]).then_inc(sh, 16)
        e.dma_start(out=bout_sb[:], in_=b_out[:]).then_inc(sh, 16)
        e.dma_start(out=b1L_sb[:], in_=b1L[:]).then_inc(sh, 16)
        e.dma_start(out=b2L_sb[:], in_=b2L[:]).then_inc(sh, 16)
        e.dma_start(out=gam_sb[:], in_=gam[:]).then_inc(sh, 16)
        fin(e, e.dma_start(out=bet_sb[:], in_=bet[:]))
    B.step("sync", "sLD", pre_sync, n_dma=10,
           writes=("x", "tri", "flag", "bqkv", "bout", "b1L", "b2L", "gam", "bet", "ident"))

    def pre_g(e):
        sh = B.cur[0]
        e.dma_start(out=ind_b[:], in_=ind_in[:]).then_inc(sh, 16)
        e.dma_start(out=indtb[:], in_=indT_in[:]).then_inc(sh, 16)
        fin(e, e.dma_start(out=t8a[:, 0:32], in_=b1[:]))
    B.step("gpsimd", "sGD", pre_g, n_dma=3, writes=("ind", "indtb", "b1q8"))

    def pre_v(e):
        e.memset(ones_b[:], 1.0)
        e.memset(ones_f[:], 1.0)
        e.tensor_copy(ident_b[:], idf_sb[:])
        e.tensor_scalar(fli_sb[:], flag_sb[:], -1.0, 1.0, AL.mult, AL.add)
        e.memset(ccol[:, 0:1], B_N2EPS)
        e.memset(ccol[:, 1:2], LN1024)
        e.memset(ccol[:, 2:3], 1e-5)
        e.memset(ccol[:, 3:4], -LN127)
        e.memset(ccol[:, 4:5], LN127)
        e.memset(ccol[:, 5:6], float(np.float32(4096.0 * 1e-5)))
        e.memset(ccol[:, 6:7], float(np.log(np.float64(4096.0)) * 0.5))
        e.memset(ccol[:, 7:8], 1e-6)
        e.tensor_copy(b1q_sb[:], t8a[:, 0:32])
        fin(e, e.engine_nop())
    B.step("vector", "sV", pre_v, reads=("b1q8", "ident", "flag"),
           writes=("ones", "b1q", "identb", "fli"))

    def pre_g2(e):
        fin(e, e.dma_start(out=t8a[:, 0:8], in_=b2[:]))
    B.step("gpsimd", "sGD", pre_g2, reads=("b1q",), writes=("b2q8",))

    def pre_v2(e):
        e.tensor_copy(b2q_sb[:], t8a[:, 0:8])
        fin(e, e.engine_nop())
    B.step("vector", "sV", pre_v2, reads=("b2q8",), writes=("b2q",))


    def emit_pmax_bcast(bufname, buf, tau, op, dst_psname, dst_ps):
        nblk = tau // 128
        for j in range(nblk):
            B.step("tensor", "sMM",
                   lambda e, j=j: fin(e, e.transpose(
                       ps[7][:, 0:128], buf[0:128, _ts(j, 128)], idf_sb[:])),
                   reads=(bufname, "ident"), writes=("ps7",))
            B.step("vector", "sV",
                   lambda e, j=j: fin(e, e.tensor_reduce(
                       pm_col[:, j:j + 1], ps[7][:, 0:128], AX.X, op)),
                   reads=("ps7",), writes=("pmcol",))
        for j in range(nblk):
            B.step("tensor", "sMM",
                   lambda e, j=j: fin(e, e.transpose(
                       ps[7][0:1, 0:128], pm_col[:, j:j + 1], idf_sb[:])),
                   reads=("pmcol", "ident"), writes=("ps7",))
            B.step("vector", "sV",
                   lambda e, j=j: fin(e, e.tensor_copy(
                       pm_row[0:1, _ts(j, 128)], ps[7][0:1, 0:128])),
                   reads=("ps7",), writes=("pmrow",))
        B.step("tensor", "sMM",
               lambda e: fin(e, e.matmul(dst_ps[:, 0:tau], ones_f[0:1, :],
                                         pm_row[0:1, 0:tau], start=True, stop=True)),
               reads=("pmrow", "ones"), writes=(dst_psname,))

    def emit_wstats(C):
        B.new_epoch()
        scr8 = C.f32(4096)
        wb = [C.f32(4096), C.f32(4096)]
        B.step("vector", "sV", lambda e: fin(e, e.memset(wsa[:, 0:2], 0.0)),
               writes=("wsa",))
        pend = [None]
        kslices = [(wi, wsrc, sl, ncols)
                   for wi, wsrc, nsl, ncols in ((0, w1L, 8, 4096), (1, w2L, 32, 1024))
                   for sl in range(nsl)]
        for gi, (wi, wsrc, sl, ncols) in enumerate(kslices):
            buf = wb[gi % 2]
            res = f"wstb{gi % 2}"
            B.step("sync", "sLD",
                   lambda e, b=buf, w=wsrc, s=sl, nc_=ncols: fin(e, e.dma_start(
                       out=b[:, 0:nc_], in_=w.tensor.ap()[_ts(s, 128), :])),
                   writes=(res,))

            def f_add(e, b=buf, nc_=ncols, slot=2 + gi % 2, pd=pend[0]):
                s8 = scr8[:, 0:nc_]
                bb = b[:, 0:nc_]
                e.tensor_scalar(s8, bb, -1.0, None, AL.mult)
                e.tensor_tensor(s8, s8, bb, AL.max)
                if pd is not None:
                    pwi, pslot = pd
                    e.tensor_tensor(wsa[:, pwi:pwi + 1], wsa[:, pwi:pwi + 1],
                                    wsa[:, pslot:pslot + 1], AL.add)
                fin(e, e.tensor_reduce(wsa[:, slot:slot + 1], s8, AX.X, AL.add))
            B.step("vector", "sV", f_add, reads=(res, "wsa"), writes=("wsa", res))
            pend[0] = (wi, 2 + gi % 2)

        def f_drain(e, pd=pend[0]):
            e.tensor_scalar(scr8[:], scr8[:], 1.0, None, AL.mult)
            e.tensor_scalar(scr8[:], scr8[:], 1.0, None, AL.mult)
            pwi, pslot = pd
            fin(e, e.tensor_tensor(wsa[:, pwi:pwi + 1], wsa[:, pwi:pwi + 1],
                                   wsa[:, pslot:pslot + 1], AL.add))
        B.step("vector", "sV", f_drain, reads=("wsa",), writes=("wsa",))

        for c in (0, 1):
            B.step("tensor", "sMM",
                   lambda e, c=c: fin(e, e.transpose(
                       ps[7][0:1, 0:128], wsa[:, c:c + 1], idf_sb[:])),
                   reads=("wsa", "ident"), writes=("ps7",))
            B.step("vector", "sV",
                   lambda e, c=c: fin(e, e.tensor_reduce(
                       wsa[0:1, c:c + 1], ps[7][0:1, 0:128], AX.X, AL.add)),
                   reads=("ps7",), writes=("wsarow",) if c == 1 else ())

        def f_bc(e):
            fin(e, e.matmul(ps[4][:, 0:2], ones_f[0:1, :], wsa[0:1, 0:2],
                            start=True, stop=True))
        B.step("tensor", "sMM", f_bc, reads=("wsarow", "ones"), writes=("ps4",))

        def f_fin(e):
            e.tensor_scalar(mw_sb[:], ps[4][:, 0:2], 1.0 / 4194304.0, 1e-5,
                            AL.mult, AL.max)
            fin(e, e.engine_nop())
        B.step("vector", "sV", f_fin, reads=("ps4",), writes=("mw",))

        def f_sw(e):
            e.activation(sw_sb[:], mw_sb[:], ACT.Ln)
            fin(e, e.activation(sw_sb[:], sw_sb[:], ACT.Exp, scale=-1.0))
        B.step("scalar", "sS", f_sw, reads=("mw",), writes=("sw",))

    def mk_stats_bufs(C, tau, want_r=True):
        d = dict(
            stA=C.f32(tau), stB=C.f32(tau),
            m_t=C.f32(tau), s_t=C.f32(tau),
        )
        if want_r:
            d["r_t"] = C.f32(tau)
        return d

    def emit_stats(SB, c0, tau, want_r=False, src=None, srcres="x"):
        for c in range(NCH):
            xs = src(c) if src is not None else x_sb[:, c * T + c0: c * T + c0 + tau]
            xr = xb_g[c % 2][:, 0:tau]
            sr = sqb_g[c % 2][:, 0:tau]
            B.step("vector", "sV",
                   lambda e, xs=xs, xr=xr: fin(e, e.tensor_copy(xr, xs)),
                   reads=(srcres,), writes=(f"xr{c % 2}",))
            B.step("scalar", "sS",
                   lambda e, xs=xs, sr=sr: fin(e, e.activation(sr, xs, ACT.Square)),
                   reads=(srcres,), writes=(f"sqr{c % 2}",))

            def f_mm(e, c=c, xr=xr, sr=sr, tau=tau):
                e.matmul(ps[2][:, 0:tau], ones_b[:], xr,
                         start=(c == 0), stop=(c == NCH - 1))
                fin(e, e.matmul(ps[3][:, 0:tau], ones_b[:], sr,
                                start=(c == 0), stop=(c == NCH - 1)))
            B.step("tensor", "sMM", f_mm,
                   reads=(f"xr{c % 2}", f"sqr{c % 2}", "ones"),
                   writes=("ps2", "ps3") if c == 0 else ())
        mm_done = (B.sems["sMM"], B.sems["sMM"].n)
        B.ready["ps2"] = mm_done
        B.ready["ps3"] = mm_done
        B.readers["ps2"] = []
        B.readers["ps3"] = []

        stA, stB = SB["stA"][:, 0:tau], SB["stB"][:, 0:tau]
        mdst, sdst = SB["m_t"][:, 0:tau], SB["s_t"][:, 0:tau]

        def f_fin1(e):
            e.tensor_scalar(mdst, ps[2][:, 0:tau], 1.0 / 1024.0, None, AL.mult)
            e.tensor_tensor(stA, mdst, mdst, AL.mult)
            fin(e, e.scalar_tensor_tensor(stA, ps[3][:, 0:tau], 1.0 / 1024.0, stA,
                                          AL.mult, AL.subtract))
        B.step("vector", "sV", f_fin1, reads=("ps2", "ps3", "stA"),
               writes=("m_t", "stA"))

        def f_fin2(e):
            e.activation(stB, stA, ACT.Ln, bias=ccol[:, 2:3])
            fin(e, e.activation(sdst, stB, ACT.Exp, scale=-0.5))
        B.step("scalar", "sS", f_fin2, reads=("stA", "stB"), writes=("s_t", "stB"))

        if want_r:
            r_t = SB["r_t"][:, 0:tau]
            B.step("scalar", "sS",
                   lambda e: fin(e, e.activation(stA, stA, ACT.Ln)),
                   reads=("stA",), writes=("stA",), extra=(B.ready.get("s_t"),))
            B.step("vector", "sV",
                   lambda e: fin(e, e.tensor_tensor(stA, stA, stB, AL.subtract)),
                   reads=("stA", "stB"), writes=("stA",))

            def f_r3(e):
                e.activation(stA, stA, ACT.Exp)
                e.activation(stB, stA, ACT.Ln, bias=ccol[:, 2:3])
                fin(e, e.activation(r_t, stB, ACT.Exp, scale=-0.5))
            B.step("scalar", "sS", f_r3, reads=("stA", "stB"),
                   writes=("r_t", "stA", "stB"))

    def emit_l0ffn(tag, phases=(1, 2)):
        TAU = 512
        NT = T // TAU
        if tag == "dbg":
            NT = 1
        B.new_epoch()
        C = Carver()
        wres = C.fp8(8 * 4096)
        SB = mk_stats_bufs(C, TAU, want_r=False)
        xq8 = C.fp8(NCH * TAU)
        hq8t = C.fp8(32 * TAU) if tag != "dbg" else None
        hrot = [C.bf16(TAU), C.bf16(TAU)]
        dt1 = C.f32(TAU)

        B.step("gpsimd", "sGD",
               lambda e: fin(e, e.dma_start(
                   out=wres.rearrange("p (c f) -> p c f", c=NCH),
                   in_=W1.tensor.ap().rearrange("(c p) f -> p c f", p=128))),
               writes=("wres",))

        for ti in range(NT):
            c0 = ti * TAU
            emit_stats(SB, c0, TAU)
            for c in range(NCH):
                xs = x_sb[:, c * T + c0: c * T + c0 + TAU]

                def f_xq(e, xs=xs, c=c):
                    e.tensor_tensor(dt1, xs, SB["m_t"][:, 0:TAU], AL.subtract)
                    fin(e, e.tensor_tensor(xq8[:, _ts(c, TAU)], dt1,
                                           SB["s_t"][:, 0:TAU], AL.mult))
                B.step("vector", "sV", f_xq, reads=("x", "m_t", "s_t", "dt1"),
                       writes=("xq8", "dt1") if c == 0 else ("dt1",))
            xq_done = (B.sems["sV"], B.sems["sV"].n)
            B.ready["xq8"] = xq_done

            if tag == "dbg" and ti == 0:
                B.step("sync", "sST",
                       lambda e: fin(e, e.dma_start(out=dbg_d[:, 0:TAU], in_=SB["m_t"][:, 0:TAU])),
                       reads=("m_t",), writes=("dbgo",))
                B.step("sync", "sST",
                       lambda e: fin(e, e.dma_start(out=dbg_d[:, TAU:2*TAU], in_=SB["s_t"][:, 0:TAU])),
                       reads=("s_t",), writes=("dbgo",))
                B.step("vector", "sV",
                       lambda e: fin(e, e.tensor_copy(d_dbg_f32(), xq8[:])),
                       reads=("xq8",), writes=("dbg1",))
                B.step("sync", "sST",
                       lambda e: fin(e, e.dma_start(out=dbg_d[:, 2*TAU:2*TAU+NCH*TAU], in_=d_dbg_f32())),
                       reads=("dbg1",), writes=("dbgo",))
                B.step("vector", "sV",
                       lambda e: fin(e, e.tensor_copy(d_dbg_f32()[:, 0:128], ones_b[:])),
                       reads=("ones",), writes=("dbg1",))
                B.step("vector", "sV",
                       lambda e: fin(e, e.tensor_copy(d_dbg_f32()[:, 128:640], xb_g[1][:])),
                       reads=("xr1",), writes=("dbg1",))
                B.step("sync", "sST",
                       lambda e: fin(e, e.dma_start(out=dbg_d[:, 2*TAU+NCH*TAU:2*TAU+NCH*TAU+640], in_=d_dbg_f32()[:, 0:640])),
                       reads=("dbg1",), writes=("dbgo",))
                return

            for F in range(32):
                sl = F % 2

                def f_mm(e, F=F, sl=sl):
                    for k in range(8):
                        inst = e.matmul(ps[sl][:, 0:TAU],
                                        wres[:, k * 4096 + F * 128: k * 4096 + F * 128 + 128],
                                        xq8[:, _ts(k, TAU)], start=(k == 0), stop=(k == 7))
                    fin(e, inst)
                B.step("tensor", "sMM", f_mm, reads=("wres", "xq8"),
                       writes=(f"ps{sl}",))

                def f_ev(e, F=F, sl=sl):
                    fin(e, e.activation(hrot[sl], ps[sl][:, 0:TAU], ACT.Relu,
                                        bias=b1q_sb[:, F:F + 1]))
                B.step("scalar", "sS", f_ev, reads=(f"ps{sl}", "b1q"),
                       writes=(f"hrot{sl}",))

                def f_c8(e, F=F, sl=sl):
                    fin(e, e.tensor_copy(hq8t[:, _ts(F, TAU)], hrot[sl]))
                B.step("vector", "sV", f_c8, reads=(f"hrot{sl}",),
                       writes=("hq8t",) if F == 0 else ())
            hq_done = (B.sems["sV"], B.sems["sV"].n)
            B.ready["hq8t"] = hq_done

            B.step("sync", "sST",
                   lambda e, ti=ti: fin(e, e.dma_start(
                       out=sp_h8[:, ti * 32 * TAU: (ti + 1) * 32 * TAU], in_=hq8t)),
                   reads=("hq8t",), writes=("sp_h8",))

        if 2 not in phases:
            return
        B.new_epoch()
        C2 = Carver()
        wres2 = C2.fp8(32 * 1024)
        hq8t2 = C2.fp8(32 * TAU)
        mask_t = [C2.bf16(TAU), C2.bf16(TAU)]
        yrot = [C2.bf16(TAU), C2.bf16(TAU)]
        evt = C2.f32(TAU)

        B.step("gpsimd", "sGD",
               lambda e: fin(e, e.dma_start(
                   out=wres2.rearrange("p (c f) -> p c f", c=32),
                   in_=W2.tensor.ap().rearrange("(c p) f -> p c f", p=128))),
               writes=("wres",), extra=(B.ready.get("hq8t"),))

        for ti in range(NT):
            c0 = ti * TAU
            h_done = B.step("sync", "sLD",
                   lambda e, ti=ti: fin(e, e.dma_start(
                       out=hq8t2, in_=sp_h8[:, ti * 32 * TAU: (ti + 1) * 32 * TAU])),
                   reads=("sp_h8",), writes=("hq8t2",))

            for G in range(8):
                sl = G % 2
                B.step("sync", "sLD",
                       lambda e, G=G, c0=c0, sl=sl: fin(e, e.dma_start(
                           out=mask_t[sl],
                           in_=mask_in[:, G * T + c0: G * T + c0 + TAU])),
                       writes=(f"mask_{sl}",))

                def f_mm(e, G=G, sl=sl):
                    for k in range(32):
                        inst = e.matmul(ps[4 + sl][:, 0:TAU],
                                        wres2[:, k * 1024 + G * 128: k * 1024 + G * 128 + 128],
                                        hq8t2[:, _ts(k, TAU)], start=(k == 0), stop=(k == 31))
                    fin(e, inst)
                B.step("tensor", "sMM", f_mm, reads=("wres", "hq8t2"),
                       writes=(f"ps{4 + sl}",))

                def f_ev(e, G=G, sl=sl):
                    fin(e, e.activation(yrot[sl], ps[4 + sl][:, 0:TAU], ACT.Identity,
                                        bias=b2q_sb[:, G:G + 1]))
                B.step("scalar", "sS", f_ev, reads=(f"ps{4 + sl}", "b2q"),
                       writes=(f"yrot{sl}",))

                xs = x_sb[:, G * T + c0: G * T + c0 + TAU]

                def f_res(e, G=G, sl=sl, xs=xs):
                    e.tensor_tensor(evt, yrot[sl], mask_t[sl], AL.mult)
                    fin(e, e.tensor_tensor(xs, xs, evt, AL.add))
                B.step("vector", "sV", f_res,
                       reads=(f"yrot{sl}", f"mask_{sl}", "evt"), writes=("x", "evt"))

    def emit_wquant():
        B.new_epoch()
        C = Carver()
        wld = [C.f32(4096), C.f32(4096)]
        wq8 = [C.fp8(4096), C.fp8(4096)]
        for wi, wsrc, dst, nsl, ncols in ((0, w1L, sp_w1q, 8, 4096),
                                          (1, w2L, sp_w2q, 32, 1024)):
            for sl in range(nsl):
                bi = sl % 2
                B.step("sync", "sLD",
                       lambda e, b=wld[bi], w=wsrc, s=sl, nc_=ncols: fin(
                           e, e.dma_start(out=b[:, 0:nc_],
                                          in_=w.tensor.ap()[_ts(s, 128), :])),
                       writes=(f"wld_{bi}",))

                def f_q(e, bi=bi, wi=wi, nc_=ncols):
                    b = wld[bi][:, 0:nc_]
                    q = wq8[bi][:, 0:nc_]
                    e.tensor_scalar(b, b, sw_sb[:, wi:wi + 1], None, AL.mult)
                    e.tensor_scalar(b, b, MAGIC, MAGIC, AL.add, AL.subtract)
                    fin(e, e.tensor_scalar(q, b, 1.0, -1.0, AL.min, AL.max))
                B.step("vector", "sV", f_q, reads=(f"wld_{bi}", "sw"),
                       writes=(f"wld_{bi}", f"wq8_{bi}"))

                B.step("sync", "sST",
                       lambda e, bi=bi, d=dst, s=sl, nc_=ncols: fin(
                           e, e.dma_start(out=d[:, s * nc_:(s + 1) * nc_],
                                          in_=wq8[bi][:, 0:nc_])),
                       reads=(f"wq8_{bi}",), writes=("spwq",))

    def emit_l1ffn(tag):
        TAU = 256
        NT = T // TAU
        B.new_epoch()
        C = Carver()
        wres = C.fp8(8 * 4096)
        SB = mk_stats_bufs(C, TAU, want_r=True)
        kx = C.bf16(NCH * TAU)
        hbf = C.bf16(32 * TAU)
        hmax = C.bf16(TAU)
        hmin = C.bf16(TAU)
        rho = C.f32(TAU)
        rt1 = C.f32(TAU)
        rho2i = C.f32(TAU)
        srm = C.f32(TAU)
        evt = C.f32(TAU)
        evt2 = C.f32(TAU)
        dt1 = C.f32(TAU)

        B.step("sync", "sLD",
               lambda e: fin(e, e.dma_start(out=wres, in_=sp_w1q[:])),
               reads=("spwq",), writes=("wres",))

        for ti in range(NT):
            c0 = ti * TAU
            emit_stats(SB, c0, TAU, want_r=True)

            def f_mx(e, c0=c0):
                src3 = x_sb[:].rearrange("p (c t) -> p c t", c=NCH)[:, :, c0:c0 + TAU]
                src3 = src3.transpose([0, 2, 1])
                e.tensor_reduce(evt, src3, AX.X, AL.max)
                fin(e, e.tensor_reduce(evt2, src3, AX.X, AL.min))
            B.step("vector", "sV", f_mx, reads=("x", "evt", "evt2"),
                   writes=("evt", "evt2"))

            emit_pmax_bcast("evt", evt, TAU, AL.max, "ps6", ps[6])

            def f_mxa(e):
                fin(e, e.tensor_tensor(srm, ps[6][:, 0:TAU], SB["m_t"][:, 0:TAU],
                                       AL.subtract))
            B.step("vector", "sV", f_mxa, reads=("ps6", "m_t", "srm"), writes=("srm",))

            emit_pmax_bcast("evt2", evt2, TAU, AL.min, "ps6", ps[6])

            def f_mxc(e):
                e.scalar_tensor_tensor(SB["stA"][:, 0:TAU], ps[6][:, 0:TAU], -1.0,
                                       SB["m_t"][:, 0:TAU], AL.mult, AL.add)
                e.tensor_tensor(SB["stA"][:, 0:TAU], SB["stA"][:, 0:TAU], srm, AL.max)
                e.tensor_tensor(srm, SB["s_t"][:, 0:TAU], SB["r_t"][:, 0:TAU], AL.mult)
                e.tensor_tensor(SB["stA"][:, 0:TAU], SB["stA"][:, 0:TAU], srm, AL.mult)
                fin(e, e.tensor_scalar(SB["stA"][:, 0:TAU], SB["stA"][:, 0:TAU],
                                       1e-5, None, AL.max))
            B.step("vector", "sV", f_mxc, reads=("ps6", "s_t", "r_t", "stA", "srm", "m_t"),
                   writes=("stA", "srm"))

            def f_sc(e):
                e.activation(evt, SB["stA"][:, 0:TAU], ACT.Ln)
                e.activation(evt2, evt, ACT.Exp, scale=-1.0, bias=ccol[:, 4:5])
                fin(e, e.activation(evt, evt, ACT.Exp, bias=ccol[:, 3:4]))
            B.step("scalar", "sS", f_sc, reads=("stA", "evt", "evt2"),
                   writes=("evt", "evt2"))

            def f_rho(e):
                e.tensor_tensor(rho, srm, evt2, AL.mult)
                fin(e, e.tensor_scalar(rt1, evt, mw_sb[:, 0:1], None, AL.mult))
            B.step("vector", "sV", f_rho, reads=("srm", "evt", "evt2", "rho", "rt1", "mw"),
                   writes=("rho", "rt1"))

            for c in range(NCH):
                xs = x_sb[:, c * T + c0: c * T + c0 + TAU]

                def f_q(e, xs=xs, c=c):
                    e.tensor_tensor(dt1, xs, SB["m_t"][:, 0:TAU], AL.subtract)
                    e.tensor_tensor(dt1, dt1, rho, AL.mult)
                    fin(e, e.tensor_scalar(kx[:, _ts(c, TAU)], dt1, MAGIC, MAGIC,
                                           AL.add, AL.subtract))
                B.step("vector", "sV", f_q, reads=("x", "m_t", "rho", "dt1"),
                       writes=("kx", "dt1") if c == 0 else ("dt1",))
            kx_done = (B.sems["sV"], B.sems["sV"].n)
            B.ready["kx"] = kx_done

            for F in range(32):
                sl = F % 2

                def f_mm(e, F=F, sl=sl):
                    for k in range(8):
                        inst = e.matmul(ps[sl][:, 0:TAU],
                                        wres[:, k * 4096 + F * 128: k * 4096 + F * 128 + 128],
                                        kx[:, _ts(k, TAU)], start=(k == 0), stop=(k == 7))
                    fin(e, inst)
                B.step("tensor", "sMM", f_mm, reads=("wres", "kx"), writes=(f"ps{sl}",))

                def f_sc1(e, F=F, sl=sl):
                    fin(e, e.tensor_tensor(dt1, ps[sl][:, 0:TAU], rt1, AL.mult))
                B.step("vector", "sV", f_sc1, reads=(f"ps{sl}", "rt1", "dt1"),
                       writes=("dt1",))
                ps_free = (B.sems["sV"], B.sems["sV"].n)
                B.ready[f"ps{sl}"] = B.ready[f"ps{sl}"]
                B.readers[f"ps{sl}"].append(ps_free)

                def f_ev(e, F=F, sl=sl):
                    fin(e, e.activation(hbf[:, _ts(F, TAU)], dt1, ACT.Gelu,
                                        bias=b1L_sb[:, F:F + 1]))
                B.step("scalar", "sS", f_ev, reads=("dt1", "b1L"),
                       writes=(("hbf",) if F == 0 else ()) + (f"hbfF{sl}",),
                       extra=(B.ready.get("hbf"),) if F == 0 else ())

                def f_sq(e, F=F, sl=sl):
                    fin(e, e.activation(sqb_g[sl][:, 0:TAU], hbf[:, _ts(F, TAU)],
                                        ACT.Square))
                B.step("scalar", "sS", f_sq, reads=(f"hbfF{sl}",), writes=(f"sqr{sl}",))

                def f_hmm(e, F=F, sl=sl):
                    fin(e, e.matmul(ps[3][:, 0:TAU], ones_b[:], sqb_g[sl][:, 0:TAU],
                                    start=(F == 0), stop=(F == 31)))
                B.step("tensor", "sMM", f_hmm, reads=(f"sqr{sl}", "ones"),
                       writes=("ps3",) if F == 0 else ())
                if F == 31:
                    mmd = (B.sems["sMM"], B.sems["sMM"].n)
                    B.ready["ps3"] = mmd
                    B.readers["ps3"] = []

                def f_hex(e, F=F):
                    if F == 0:
                        e.tensor_copy(hmax, hbf[:, _ts(F, TAU)])
                        fin(e, e.tensor_copy(hmin, hbf[:, _ts(F, TAU)]))
                    else:
                        e.tensor_tensor(hmax, hmax, hbf[:, _ts(F, TAU)], AL.max)
                        fin(e, e.tensor_tensor(hmin, hmin, hbf[:, _ts(F, TAU)], AL.min))
                B.step("vector", "sV", f_hex, reads=("hmax", "hmin", f"hbfF{sl}"),
                       writes=("hmax", "hmin"))

            def f_tree(e):
                e.tensor_scalar(hmin[:, :], hmin[:, :], -1.0, None, AL.mult)
                e.tensor_tensor(hmax[:, :], hmax[:, :], hmin[:, :], AL.max)
                fin(e, e.tensor_copy(evt[:, 0:TAU], hmax[:, :]))
            B.step("vector", "sV", f_tree, reads=("hmax", "hmin", "evt"),
                   writes=("hmax", "hmin", "evt"))

            emit_pmax_bcast("evt", evt, TAU, AL.max, "ps6", ps[6])

            def f_r2(e):
                e.activation(SB["stB"][:, 0:TAU], ps[3][:, 0:TAU], ACT.Ln,
                             bias=ccol[:, 5:6])
                fin(e, e.activation(SB["stB"][:, 0:TAU], SB["stB"][:, 0:TAU], ACT.Exp,
                                    scale=-0.5, bias=ccol[:, 6:7]))
            B.step("scalar", "sS", f_r2, reads=("ps3", "stB"), writes=("stB",))

            def f_mx2(e):
                e.tensor_tensor(SB["stA"][:, 0:TAU], ps[6][:, 0:TAU],
                                SB["stB"][:, 0:TAU], AL.mult)
                fin(e, e.tensor_scalar(SB["stA"][:, 0:TAU], SB["stA"][:, 0:TAU],
                                       1e-5, None, AL.max))
            B.step("vector", "sV", f_mx2, reads=("ps6", "stB", "stA"), writes=("stA",))

            def f_sc2(e):
                e.activation(evt, SB["stA"][:, 0:TAU], ACT.Ln)
                e.activation(rho2i, evt, ACT.Exp, bias=ccol[:, 3:4])
                fin(e, e.activation(evt2, evt, ACT.Exp, scale=-1.0, bias=ccol[:, 4:5]))
            B.step("scalar", "sS", f_sc2, reads=("stA", "evt", "evt2", "rho2i"),
                   writes=("rho2i", "evt", "evt2"))

            def f_rr(e):
                fin(e, e.tensor_tensor(evt, SB["stB"][:, 0:TAU], evt2, AL.mult))
            B.step("vector", "sV", f_rr, reads=("stB", "evt2", "evt"), writes=("evt",))

            for F in range(32):
                def f_q2(e, F=F):
                    e.tensor_tensor(dt1, hbf[:, _ts(F, TAU)], evt, AL.mult)
                    fin(e, e.tensor_scalar(hbf[:, _ts(F, TAU)], dt1, MAGIC, MAGIC,
                                           AL.add, AL.subtract))
                B.step("vector", "sV", f_q2, reads=("hbf", "evt", "dt1"),
                       writes=("dt1", "hbf") if F == 31 else ("dt1",))
            k2_done = (B.sems["sV"], B.sems["sV"].n)
            B.ready["hbf"] = k2_done

            B.step("sync", "sST",
                   lambda e, ti=ti: fin(e, e.dma_start(
                       out=sp_k2[:, ti * 32 * TAU: (ti + 1) * 32 * TAU], in_=hbf)),
                   reads=("hbf",), writes=("sp_k2",))
            B.step("sync", "sST",
                   lambda e, ti=ti: fin(e, e.dma_start(
                       out=sp_r2i[:, ti * TAU: (ti + 1) * TAU], in_=rho2i)),
                   reads=("rho2i",), writes=("sp_r2i",))

        B.new_epoch()
        C2 = Carver()
        wres2 = C2.fp8(32 * 1024)
        k2t = C2.bf16(32 * TAU)
        r2it = C2.f32(TAU)
        evt_b2 = C2.f32(TAU)

        B.step("sync", "sLD",
               lambda e: fin(e, e.dma_start(out=wres2, in_=sp_w2q[:])),
               reads=("spwq",), writes=("wres",), extra=(B.ready.get("hbf"),))

        for ti in range(NT):
            c0 = ti * TAU
            B.step("sync", "sLD",
                   lambda e, ti=ti: fin(e, e.dma_start(
                       out=k2t, in_=sp_k2[:, ti * 32 * TAU: (ti + 1) * 32 * TAU])),
                   reads=("sp_k2",), writes=("k2t",))
            B.step("sync", "sLD",
                   lambda e, ti=ti: fin(e, e.dma_start(
                       out=r2it, in_=sp_r2i[:, ti * TAU: (ti + 1) * TAU])),
                   reads=("sp_r2i",), writes=("r2it",))

            for G in range(8):
                sl = G % 2

                def f_mm(e, G=G, sl=sl):
                    for k in range(32):
                        inst = e.matmul(ps[4 + sl][:, 0:TAU],
                                        wres2[:, k * 1024 + G * 128: k * 1024 + G * 128 + 128],
                                        k2t[:, _ts(k, TAU)], start=(k == 0), stop=(k == 31))
                    fin(e, inst)
                B.step("tensor", "sMM", f_mm, reads=("wres", "k2t"),
                       writes=(f"ps{4 + sl}",))

                xs = x_sb[:, G * T + c0: G * T + c0 + TAU]

                def f_u(e, G=G, sl=sl, xs=xs):
                    e.tensor_tensor(evt_b2, ps[4 + sl][:, 0:TAU], r2it, AL.mult)
                    e.tensor_scalar(evt_b2, evt_b2, mw_sb[:, 1:2], None, AL.mult)
                    e.tensor_scalar(evt_b2, evt_b2, b2L_sb[:, G:G + 1], 0.5, AL.add, AL.mult)
                    fin(e, e.tensor_tensor(xs, xs, evt_b2, AL.add))
                B.step("vector", "sV", f_u,
                       reads=(f"ps{4 + sl}", "r2it", "mw", "b2L", "evt_b2"),
                       writes=("x", "evt_b2"))


    def mark_mm_ready(*names):
        mmd = (B.sems["sMM"], B.sems["sMM"].n)
        for nm in names:
            B.ready[nm] = mmd
            B.readers[nm] = []

    def emit_xnb(SB, dst, scr, c0, tau, dres):
        for cc_ in range(NCH):
            xs = x_sb[:, cc_ * T + c0: cc_ * T + c0 + tau]

            def f_n(e, xs=xs, cc_=cc_):
                e.tensor_tensor(scr[:, 0:tau], xs, SB["m_t"][:, 0:tau], AL.subtract)
                fin(e, e.tensor_tensor(dst[:, cc_ * tau:(cc_ + 1) * tau], scr[:, 0:tau],
                                       SB["s_t"][:, 0:tau], AL.mult))
            B.step("vector", "sV", f_n, reads=("x", "m_t", "s_t", "scr"),
                   writes=(dres, "scr"))

    def emit_l1attn():
        TAU = 256
        NT = T // TAU

        def pmm(wr, xn, mc, tau, sl, wres_name, xname):
            def f_mm(e, mc=mc, sl=sl, tau=tau):
                for kc in range(8):
                    inst = e.matmul(ps[sl][:, 0:tau],
                                    wr[:, kc * 1024 + mc * 128: kc * 1024 + mc * 128 + 128],
                                    xn[:, _ts(kc, tau)], start=(kc == 0), stop=(kc == 7))
                fin(e, inst)
            B.step("tensor", "sMM", f_mm, reads=(wres_name, xname), writes=(f"ps{sl}",))
            mark_mm_ready(f"ps{sl}")

        B.new_epoch()
        C = Carver()
        wkr = C.bf16(8 * 1024)
        SB = mk_stats_bufs(C, TAU, want_r=False)
        kf = C.f32(8 * TAU)
        xnb = C.bf16(8 * TAU)
        eA = C.f32(TAU); eB = C.f32(TAU); eC = C.f32(TAU)
        scr = C.f32(TAU)
        ksa = C.f32(128)
        ksl = C.f32(8)

        B.step("gpsimd", "sGD",
               lambda e: fin(e, e.dma_start(
                   out=wkr.rearrange("p (c f) -> p c f", c=8),
                   in_=Wk.tensor.ap().rearrange("(c p) f -> p c f", p=128))),
               writes=("wres",))
        for ti in range(NT):
            c0 = ti * TAU
            emit_stats(SB, c0, TAU)
            emit_xnb(SB, xnb, scr, c0, TAU, "xnb")
            for mc in range(8):
                sl = mc % 2
                pmm(wkr, xnb, mc, TAU, sl, "wres", "xnb")
                B.step("vector", "sV",
                       lambda e, sl=sl: fin(e, e.tensor_scalar(
                           eA, ps[sl][:, 0:TAU], 0.0, None, AL.min)),
                       reads=(f"ps{sl}", "eA"), writes=("eA",))
                B.step("scalar", "sS",
                       lambda e: fin(e, e.activation(eB, eA, ACT.Exp)),
                       reads=("eA", "eB"), writes=("eB",))

                def f_e2(e, sl=sl, mc=mc, ti=ti):
                    e.tensor_scalar(eC, ps[sl][:, 0:TAU], 0.0, None, AL.max)
                    e.tensor_tensor(kf[:, _ts(mc, TAU)], eB, eC, AL.add)
                    fin(e, e.tensor_reduce(ksa[:, mc * 16 + ti: mc * 16 + ti + 1],
                                           kf[:, _ts(mc, TAU)], AX.X, AL.add))
                B.step("vector", "sV", f_e2, reads=(f"ps{sl}", "eB", "eC", "kf", "ksa"),
                       writes=("eC", "kf", "ksa"))

        def f_fin(e):
            e.memset(eA, 0.0)
            e.memset(eB, 0.0)
            fin(e, e.tensor_reduce(ksl[:, 0:8],
                                   ksa.rearrange("p (c t) -> p c t", c=8), AX.X, AL.add))
        B.step("vector", "sV", f_fin, reads=("ksa", "eA", "eB", "ksl"),
               writes=("ksl", "eA", "eB"))
        B.step("sync", "sST", lambda e: fin(e, e.dma_start(out=ar_in[:], in_=ksl[:, 0:8])),
               reads=("ksl",), writes=("arin",))
        B.step("gpsimd", "sG",
               lambda e: fin(e, e.collective_compute(
                   "AllReduce", AL.add, replica_groups=RG,
                   ins=[ar_in[:]], outs=[ar_out[:]])),
               reads=("arin",), writes=("arout",))
        B.step("sync", "sLD", lambda e: fin(e, e.dma_start(out=ks_sb[:], in_=ar_out[:])),
               reads=("arout",), writes=("ks",))

        B.new_epoch()
        C2 = Carver()
        wres5 = C2.bf16(8 * 1024)
        SB2 = mk_stats_bufs(C2, TAU, want_r=False)
        qb = C2.bf16(8 * TAU)
        kb = C2.bf16(8 * TAU)
        vf = C2.f32(8 * TAU)
        xnb2 = C2.bf16(8 * TAU)
        eA2 = C2.f32(TAU); eB2 = C2.f32(TAU); eC2 = C2.f32(TAU)
        scr2 = C2.f32(TAU)
        prodb = C2.bf16(TAU)
        rhob = C2.bf16(TAU)
        rf = C2.f32(TAU)

        def ld_w5(Wsrc):
            B.step("gpsimd", "sGD",
                   lambda e, Wsrc=Wsrc: fin(e, e.dma_start(
                       out=wres5.rearrange("p (c f) -> p c f", c=8),
                       in_=Wsrc.tensor.ap().rearrange("(c p) f -> p c f", p=128))),
                   writes=("wres",))

        def elu_bf(mc, sl, dst, dres):
            B.step("vector", "sV",
                   lambda e, sl=sl: fin(e, e.tensor_scalar(
                       eA2, ps[sl][:, 0:TAU], 0.0, None, AL.min)),
                   reads=(f"ps{sl}", "eA"), writes=("eA",))
            B.step("scalar", "sS",
                   lambda e: fin(e, e.activation(eB2, eA2, ACT.Exp)),
                   reads=("eA", "eB"), writes=("eB",))

            def f_e2(e, sl=sl, mc=mc):
                e.tensor_scalar(eC2, ps[sl][:, 0:TAU], 0.0, None, AL.max)
                fin(e, e.tensor_tensor(dst[:, _ts(mc, TAU)], eB2, eC2, AL.add))
            B.step("vector", "sV", f_e2, reads=(f"ps{sl}", "eB", "eC", dres),
                   writes=("eC", dres))

        for ti in range(NT):
            c0 = ti * TAU
            emit_stats(SB2, c0, TAU)
            emit_xnb(SB2, xnb2, scr2, c0, TAU, "xnb")
            ld_w5(Wq)
            for mc in range(8):
                pmm(wres5, xnb2, mc, TAU, mc % 2, "wres", "xnb")
                elu_bf(mc, mc % 2, qb, "qb")
            ld_w5(Wk)
            for mc in range(8):
                pmm(wres5, xnb2, mc, TAU, mc % 2, "wres", "xnb")
                elu_bf(mc, mc % 2, kb, "kb")
            for mc in range(8):
                B.step("vector", "sV",
                       lambda e, mc=mc: fin(e, e.tensor_tensor(
                           prodb, qb[:, _ts(mc, TAU)], kb[:, _ts(mc, TAU)], AL.mult)),
                       reads=("qb", "kb", "prodb"), writes=("prodb",))
                B.step("tensor", "sMM",
                       lambda e, mc=mc: fin(e, e.matmul(
                           ps[6][0:16, 0:TAU], ind_b[:, _ts(mc, 16)], prodb,
                           start=(mc == 0), stop=(mc == 7))),
                       reads=("prodb", "ind"), writes=("ps6",) if mc == 0 else ())
            for mc in range(8):
                B.step("vector", "sV",
                       lambda e, mc=mc: fin(e, e.tensor_scalar(
                           prodb, qb[:, _ts(mc, TAU)], ks_sb[:, mc:mc + 1], None, AL.mult)),
                       reads=("qb", "ks", "prodb"), writes=("prodb",))
                B.step("tensor", "sMM",
                       lambda e, mc=mc: fin(e, e.matmul(
                           ps[6][0:16, TAU:2 * TAU], ind_b[:, _ts(mc, 16)], prodb,
                           start=(mc == 0), stop=(mc == 7))),
                       reads=("prodb", "ind"), writes=())
            mark_mm_ready("ps6")
            B.step("scalar", "sS",
                   lambda e: fin(e, e.activation(eA2[0:16, :], ps[6][0:16, TAU:2 * TAU],
                                                 ACT.Ln, bias=ccol[0:16, 7:8])),
                   reads=("ps6", "eA"), writes=("eA",))
            B.step("scalar", "sS",
                   lambda e: fin(e, e.activation(eA2[0:16, :], eA2[0:16, :],
                                                 ACT.Exp, scale=-1.0)),
                   reads=("eA",), writes=("eA",))

            def f_rho2(e):
                e.tensor_tensor(rf[0:16, :], ps[6][0:16, 0:TAU], eA2[0:16, :], AL.mult)
                fin(e, e.tensor_copy(rhob[0:16, :], rf[0:16, :]))
            B.step("vector", "sV", f_rho2, reads=("ps6", "eA", "rf", "rhob"),
                   writes=("rf", "rhob"))

            ld_w5(Wv)
            for mc in range(8):
                sl = mc % 2
                pmm(wres5, xnb2, mc, TAU, sl, "wres", "xnb")
                B.step("vector", "sV",
                       lambda e, mc=mc, sl=sl: fin(e, e.tensor_copy(
                           vf[:, _ts(mc, TAU)], ps[sl][:, 0:TAU])),
                       reads=(f"ps{sl}", "vf"), writes=("vf",))
            for mc in range(8):
                B.step("tensor", "sMM",
                       lambda e, mc=mc: fin(e, e.matmul(
                           ps[7][:, 0:TAU], indtb[0:16, _ts(mc, 128)], rhob[0:16, 0:TAU],
                           start=True, stop=True)),
                       reads=("rhob", "indtb"), writes=("ps7",))
                mark_mm_ready("ps7")
                B.step("vector", "sV",
                       lambda e, mc=mc: fin(e, e.tensor_tensor(
                           vf[:, _ts(mc, TAU)], vf[:, _ts(mc, TAU)], ps[7][:, 0:TAU],
                           AL.mult)),
                       reads=("ps7", "vf"), writes=("vf",))
            ld_w5(Wg)
            for mc in range(8):
                sl = mc % 2
                pmm(wres5, xnb2, mc, TAU, sl, "wres", "xnb")
                B.step("vector", "sV",
                       lambda e, mc=mc, sl=sl: fin(e, e.tensor_copy(
                           kb[:, _ts(mc, TAU)], ps[sl][:, 0:TAU])),
                       reads=(f"ps{sl}", "kb"), writes=("kb",))
            emit_stats(SB2, 0, TAU, src=lambda cc_: vf[:, _ts(cc_, TAU)], srcres="vf")
            for mc in range(8):
                def f_zh(e, mc=mc):
                    zv = vf[:, _ts(mc, TAU)]
                    e.tensor_tensor(scr2, zv, SB2["m_t"][:, 0:TAU], AL.subtract)
                    e.tensor_tensor(scr2, scr2, SB2["s_t"][:, 0:TAU], AL.mult)
                    e.tensor_scalar(scr2, scr2, gam_sb[:, mc:mc + 1], None, AL.mult)
                    fin(e, e.tensor_scalar(qb[:, _ts(mc, TAU)], scr2,
                                           bet_sb[:, mc:mc + 1], None, AL.add))
                B.step("vector", "sV", f_zh,
                       reads=("vf", "m_t", "s_t", "gam", "bet", "scr", "qb"),
                       writes=("scr", "qb"))
                B.step("scalar", "sS",
                       lambda e, mc=mc: fin(e, e.activation(
                           eB2, kb[:, _ts(mc, TAU)], ACT.Sigmoid)),
                       reads=("kb", "eB"), writes=("eB",))

                def f_zg(e, mc=mc):
                    e.tensor_tensor(eC2, kb[:, _ts(mc, TAU)], eB2, AL.mult)
                    fin(e, e.tensor_tensor(xnb2[:, _ts(mc, TAU)], qb[:, _ts(mc, TAU)],
                                           eC2, AL.mult))
                B.step("vector", "sV", f_zg, reads=("kb", "eB", "eC", "qb", "xnb"),
                       writes=("eC", "xnb"))
            ld_w5(Wo)
            for mc in range(8):
                sl = mc % 2
                pmm(wres5, xnb2, mc, TAU, sl, "wres", "xnb")

                def f_out(e, mc=mc, sl=sl, c0=c0):
                    xs = x_sb[:, mc * T + c0: mc * T + c0 + TAU]
                    fin(e, e.tensor_tensor(xs, xs, ps[sl][:, 0:TAU], AL.add))
                B.step("vector", "sV", f_out, reads=(f"ps{sl}", "x"), writes=("x",))

    def emit_l0attn():
        CT = 128
        NCHK = T // CT

        B.new_epoch()
        C = Carver()
        wmv = [C.bf16(2048), C.bf16(2048)]
        SB = mk_stats_bufs(C, CT, want_r=False)
        xnb = C.bf16(NCH * CT)
        ktok = C.bf16(1024)
        vone = C.bf16(16 * 65)
        eA = C.f32(256); eB = C.f32(256); eC = C.f32(256)
        scr = C.f32(CT)
        stt = C.f32(1040)
        brow_b = C.bf16(3072)[0:1, :]
        B.step("gpsimd", "sGD",
               lambda e: fin(e, e.dma_start(out=brow_b, in_=b_qkv_row_in[:])),
               writes=("brow",))

        def f_vo(e):
            v3 = vone.rearrange("p (h f) -> p h f", h=16)
            e.memset(v3[:, :, 64:65], 1.0)
            fin(e, e.memset(stt[0:64, :], 0.0))
        B.step("vector", "sV", f_vo, writes=("vone", "stt"))

        for c in range(NCHK):
            c0 = c * CT
            emit_stats(SB, c0, CT)
            emit_xnb(SB, xnb, scr, c0, CT, "xnb")
            for kc in range(8):
                sl = kc % 2
                B.step("gpsimd", "sGD",
                       lambda e, kc=kc, sl=sl: fin(e, e.dma_start(
                           out=wmv[sl], in_=W_qkv.tensor.ap()[_ts(kc, 128), 1024:3072])),
                       writes=(f"wmv{sl}",))

                def f_mm(e, kc=kc, sl=sl):
                    for blk in range(4):
                        inst = e.matmul(ps[blk][:, 0:512], xnb[:, _ts(kc, CT)],
                                        wmv[sl][:, _ts(blk, 512)],
                                        start=(kc == 0), stop=False)
                    fin(e, inst)
                B.step("tensor", "sMM", f_mm, reads=(f"wmv{sl}", "xnb"),
                       writes=("ps0", "ps1", "ps2", "ps3") if kc == 0 else ())

            def f_biasp(e):
                for blk in range(4):
                    inst = e.matmul(ps[blk][:, 0:512], ones_b[0:1, 0:128],
                                    brow_b[0:1, 1024 + blk * 512: 1024 + blk * 512 + 512],
                                    start=False, stop=True)
                fin(e, inst)
            B.step("tensor", "sMM", f_biasp, reads=("brow", "ones"), writes=())
            mark_mm_ready("ps0", "ps1", "ps2", "ps3")
            for blk in range(4):
                bank = blk // 2
                c0b = (blk % 2) * 256
                B.step("vector", "sV",
                       lambda e, bank=bank, c0b=c0b: fin(e, e.tensor_scalar(
                           eA, ps[bank][:, c0b:c0b + 256], 0.0, None, AL.min)),
                       reads=(f"ps{bank}", "eA"), writes=("eA",))
                B.step("scalar", "sS",
                       lambda e: fin(e, e.activation(eB, eA, ACT.Exp)),
                       reads=("eA", "eB"), writes=("eB",))

                def f_k2(e, bank=bank, c0b=c0b, blk=blk):
                    e.tensor_scalar(eC, ps[bank][:, c0b:c0b + 256], 0.0, None, AL.max)
                    fin(e, e.scalar_tensor_tensor(ktok[:, blk * 256:(blk + 1) * 256],
                                                  eB, 1e-4, eC, AL.add, AL.add))
                B.step("vector", "sV", f_k2, reads=(f"ps{bank}", "eB", "eC", "ktok"),
                       writes=("eC", "ktok"))
            for h in range(16):
                bank = 2 + h // 8
                cb = (h % 8) * 64
                B.step("vector", "sV",
                       lambda e, bank=bank, cb=cb, h=h: fin(e, e.tensor_copy(
                           vone[:, h * 65: h * 65 + 64], ps[bank][:, cb:cb + 64])),
                       reads=(f"ps{bank}", "vone"), writes=("vone",))
            for h in range(16):
                bank = 4 + h % 2
                B.step("tensor", "sMM",
                       lambda e, h=h, bank=bank: fin(e, e.matmul(
                           ps[bank][0:64, 0:65], ktok[:, _ts(h, 64)],
                           vone[:, h * 65:(h + 1) * 65],
                           start=True, stop=True)),
                       reads=("ktok", "vone"), writes=(f"ps{bank}",))
                mark_mm_ready(f"ps{bank}")
                B.step("vector", "sV",
                       lambda e, h=h, bank=bank: fin(e, e.tensor_tensor(
                           stt[0:64, h * 65:(h + 1) * 65],
                           stt[0:64, h * 65:(h + 1) * 65],
                           ps[bank][0:64, 0:65], AL.add)),
                       reads=(f"ps{bank}", "stt"), writes=("stt",))

        def f_sev(e):
            fin(e, e.tensor_scalar(stt[0:64, :], stt[0:64, :], fli_sb[0:64, 0:1],
                                   None, AL.mult))
        B.step("vector", "sV", f_sev, reads=("fli", "stt"), writes=("stt",))
        B.step("sync", "sST", lambda e: fin(e, e.dma_start(out=cc_in[:], in_=stt[0:64, :])),
               reads=("stt",), writes=("ccin",))
        B.step("gpsimd", "sG",
               lambda e: fin(e, e.collective_compute(
                   "AllReduce", AL.add, replica_groups=RG,
                   ins=[cc_in[:]], outs=[cc_out[:]])),
               reads=("ccin",), writes=("ccout",))

        B.new_epoch()
        C2 = Carver()
        wmv2 = [C2.bf16(3072), C2.bf16(3072)]
        SB2 = mk_stats_bufs(C2, CT, want_r=False)
        xnb2 = C2.bf16(NCH * CT)
        qtf = C2.f32(1024)
        ktf = C2.f32(1024)
        ktok2 = C2.bf16(1024)
        vone2 = C2.bf16(16 * 65)
        attnf = C2.f32(1024)
        afm = C2.bf16(1024)
        state = C2.f32(1040)
        stateb = C2.bf16(1040)
        qfm = C2.bf16(128)
        kfm = C2.bf16(128)
        stm = C2.bf16(128)
        eA2 = C2.f32(256); eB2 = C2.f32(256); eC2 = C2.f32(256)
        scr2 = C2.f32(CT)
        rcol = C2.f32(1)
        evt5 = C2.f32(256)
        wob = [C2.bf16(1024), C2.bf16(1024)]
        oacc = C2.f32(1024)
        brow_b = C2.bf16(3072)[0:1, :]
        B.step("gpsimd", "sGD",
               lambda e: fin(e, e.dma_start(out=brow_b, in_=b_qkv_row_in[:])),
               writes=("brow",))

        B.step("sync", "sLD",
               lambda e: fin(e, e.dma_start(out=state[0:64, :], in_=cc_out[:])),
               reads=("ccout",), writes=("state",))

        def f_si(e):
            e.tensor_scalar(state[0:64, :], state[0:64, :], flag_sb[0:64, 0:1],
                            None, AL.mult)
            e.tensor_copy(stateb[0:64, :], state[0:64, :])
            v3 = vone2.rearrange("p (h f) -> p h f", h=16)
            fin(e, e.memset(v3[:, :, 64:65], 1.0))
        B.step("vector", "sV", f_si,
               reads=("state", "flag", "stateb", "vone"),
               writes=("state", "stateb", "vone"))

        for c in range(NCHK):
            c0 = c * CT
            emit_stats(SB2, c0, CT)
            emit_xnb(SB2, xnb2, scr2, c0, CT, "xnb")
            for kc in range(8):
                sl = kc % 2
                B.step("gpsimd", "sGD",
                       lambda e, kc=kc, sl=sl: fin(e, e.dma_start(
                           out=wmv2[sl], in_=W_qkv.tensor.ap()[_ts(kc, 128), :])),
                       writes=(f"wmv{sl}",))

                def f_mm(e, kc=kc, sl=sl):
                    for blk in range(6):
                        inst = e.matmul(ps[blk][:, 0:512], xnb2[:, _ts(kc, CT)],
                                        wmv2[sl][:, _ts(blk, 512)],
                                        start=(kc == 0), stop=False)
                    fin(e, inst)
                B.step("tensor", "sMM", f_mm, reads=(f"wmv{sl}", "xnb"),
                       writes=tuple(f"ps{i}" for i in range(6)) if kc == 0 else ())

            def f_bias(e):
                for blk in range(6):
                    inst = e.matmul(ps[blk][:, 0:512], ones_b[0:1, 0:128],
                                    brow_b[0:1, _ts(blk, 512)], start=False, stop=True)
                fin(e, inst)
            B.step("tensor", "sMM", f_bias, reads=("brow", "ones"), writes=())
            mark_mm_ready(*[f"ps{i}" for i in range(6)])

            for blk in range(4):
                bank = blk // 2
                c0b = (blk % 2) * 256
                B.step("vector", "sV",
                       lambda e, bank=bank, c0b=c0b: fin(e, e.tensor_scalar(
                           eA2, ps[bank][:, c0b:c0b + 256], 0.125, 0.0, AL.mult, AL.min)),
                       reads=(f"ps{bank}", "eA"), writes=("eA",))
                B.step("scalar", "sS",
                       lambda e: fin(e, e.activation(eB2, eA2, ACT.Exp)),
                       reads=("eA", "eB"), writes=("eB",))

                def f_q2(e, bank=bank, c0b=c0b, blk=blk):
                    e.tensor_scalar(eC2, ps[bank][:, c0b:c0b + 256], 0.125, 0.0,
                                    AL.mult, AL.max)
                    fin(e, e.scalar_tensor_tensor(qtf[:, blk * 256:(blk + 1) * 256],
                                                  eB2, 1e-4, eC2, AL.add, AL.add))
                B.step("vector", "sV", f_q2, reads=(f"ps{bank}", "eB", "eC", "qtok"),
                       writes=("eC", "qtok"))
            for blk in range(4):
                bank = 2 + blk // 2
                c0b = (blk % 2) * 256
                B.step("vector", "sV",
                       lambda e, bank=bank, c0b=c0b: fin(e, e.tensor_scalar(
                           eA2, ps[bank][:, c0b:c0b + 256], 0.0, None, AL.min)),
                       reads=(f"ps{bank}", "eA"), writes=("eA",))
                B.step("scalar", "sS",
                       lambda e: fin(e, e.activation(eB2, eA2, ACT.Exp)),
                       reads=("eA", "eB"), writes=("eB",))

                def f_k2(e, bank=bank, c0b=c0b, blk=blk):
                    e.tensor_scalar(eC2, ps[bank][:, c0b:c0b + 256], 0.0, None, AL.max)
                    e.scalar_tensor_tensor(ktf[:, blk * 256:(blk + 1) * 256],
                                           eB2, 1e-4, eC2, AL.add, AL.add)
                    fin(e, e.tensor_copy(ktok2[:, blk * 256:(blk + 1) * 256],
                                         ktf[:, blk * 256:(blk + 1) * 256]))
                B.step("vector", "sV", f_k2, reads=(f"ps{bank}", "eB", "eC", "ktok"),
                       writes=("eC", "ktok"))
            for h in range(16):
                bank = 4 + h // 8
                cb = (h % 8) * 64
                B.step("vector", "sV",
                       lambda e, bank=bank, cb=cb, h=h: fin(e, e.tensor_copy(
                           vone2[:, h * 65: h * 65 + 64], ps[bank][:, cb:cb + 64])),
                       reads=(f"ps{bank}", "vone"), writes=("vone",))

            for h in range(16):
                B.step("tensor", "sMM",
                       lambda e, h=h: fin(e, e.transpose(
                           ps[6][0:64, 0:128], qtf[:, _ts(h, 64)], idf_sb[:])),
                       reads=("qtok", "ident"), writes=("ps6",))
                B.step("vector", "sV",
                       lambda e: fin(e, e.tensor_copy(qfm[0:64, :], ps[6][0:64, 0:128])),
                       reads=("ps6", "qfm"), writes=("qfm",))
                B.step("tensor", "sMM",
                       lambda e, h=h: fin(e, e.transpose(
                           ps[7][0:64, 0:128], ktf[:, _ts(h, 64)], idf_sb[:])),
                       reads=("ktok", "ident"), writes=("ps7",))
                B.step("vector", "sV",
                       lambda e: fin(e, e.tensor_copy(kfm[0:64, :], ps[7][0:64, 0:128])),
                       reads=("ps7", "kfm"), writes=("kfm",))
                B.step("tensor", "sMM",
                       lambda e: fin(e, e.matmul(ps[6][:, 0:128], kfm[0:64, 0:128],
                                                 qfm[0:64, 0:128], start=True, stop=True)),
                       reads=("kfm", "qfm"), writes=("ps6",))
                mark_mm_ready("ps6")
                B.step("vector", "sV",
                       lambda e: fin(e, e.tensor_tensor(
                           stm[:, 0:128], ps[6][:, 0:128], tri_sb[:], AL.mult)),
                       reads=("ps6", "tri", "stm"), writes=("stm",))

                def f_nd(e, h=h):
                    e.matmul(ps[7][:, 0:65], stm[:, 0:128],
                             vone2[:, h * 65:(h + 1) * 65], start=True, stop=False)
                    fin(e, e.matmul(ps[7][:, 0:65], qfm[0:64, 0:128],
                                    stateb[0:64, h * 65:(h + 1) * 65],
                                    start=False, stop=True))
                B.step("tensor", "sMM", f_nd,
                       reads=("stm", "vone", "qfm", "stateb"), writes=("ps7",))
                mark_mm_ready("ps7")
                B.step("scalar", "sS",
                       lambda e: fin(e, e.activation(rcol[:, 0:1], ps[7][:, 64:65],
                                                     ACT.Ln, bias=ccol[:, 7:8])),
                       reads=("ps7", "rcol"), writes=("rcol",))
                B.step("scalar", "sS",
                       lambda e: fin(e, e.activation(rcol[:, 0:1], rcol[:, 0:1],
                                                     ACT.Exp, scale=-1.0)),
                       reads=("rcol",), writes=("rcol",))
                B.step("vector", "sV",
                       lambda e, h=h: fin(e, e.tensor_scalar(
                           attnf[:, _ts(h, 64)], ps[7][:, 0:64], rcol[:, 0:1],
                           None, AL.mult)),
                       reads=("ps7", "rcol", "attn"), writes=("attn",))
                B.step("tensor", "sMM",
                       lambda e, h=h: fin(e, e.matmul(
                           ps[4][0:64, 0:65], ktok2[:, _ts(h, 64)],
                           vone2[:, h * 65:(h + 1) * 65], start=True, stop=True)),
                       reads=("ktok", "vone"), writes=("ps4",))
                mark_mm_ready("ps4")
                B.step("vector", "sV",
                       lambda e, h=h: fin(e, e.tensor_tensor(
                           state[0:64, h * 65:(h + 1) * 65],
                           state[0:64, h * 65:(h + 1) * 65],
                           ps[4][0:64, 0:65], AL.add)),
                       reads=("ps4", "state"), writes=("state",))
            B.step("vector", "sV",
                   lambda e: fin(e, e.tensor_copy(stateb[0:64, :], state[0:64, :])),
                   reads=("state", "stateb"), writes=("stateb",))

            for fc in range(8):
                B.step("tensor", "sMM",
                       lambda e, fc=fc: fin(e, e.transpose(
                           ps[3][:, 0:128], attnf[:, _ts(fc, 128)], idf_sb[:])),
                       reads=("attn", "ident"), writes=("ps3",))
                mark_mm_ready("ps3")
                B.step("vector", "sV",
                       lambda e, fc=fc: fin(e, e.tensor_copy(
                           afm[:, _ts(fc, 128)], ps[3][:, 0:128])),
                       reads=("ps3", "afm"), writes=("afm",))
            for fc in range(8):
                sl = fc % 2
                B.step("gpsimd", "sGD",
                       lambda e, fc=fc, sl=sl: fin(e, e.dma_start(
                           out=wob[sl], in_=W_out.tensor.ap()[_ts(fc, 128), :])),
                       writes=(f"wob{sl}",))

                def f_wm(e, fc=fc, sl=sl):
                    for mc in range(8):
                        inst = e.matmul(
                            ps[mc // 4][:, (mc % 4) * 128:(mc % 4) * 128 + 128],
                            wob[sl][:, _ts(mc, 128)], afm[:, _ts(fc, 128)],
                            start=True, stop=True)
                    fin(e, inst)
                B.step("tensor", "sMM", f_wm, reads=(f"wob{sl}", "afm"),
                       writes=("ps0", "ps1"))
                mark_mm_ready("ps0", "ps1")

                def f_oa(e, fc=fc):
                    for mc in range(8):
                        reg = ps[mc // 4][:, (mc % 4) * 128:(mc % 4) * 128 + 128]
                        if fc == 0:
                            inst = e.tensor_copy(oacc[:, _ts(mc, 128)], reg)
                        else:
                            inst = e.tensor_tensor(oacc[:, _ts(mc, 128)],
                                                   oacc[:, _ts(mc, 128)], reg, AL.add)
                    fin(e, inst)
                B.step("vector", "sV", f_oa, reads=("ps0", "ps1", "oacc"),
                       writes=("oacc",))
            for mc in range(8):
                def f_oe(e, mc=mc, gc0=c * CT):
                    xs = x_sb[:, mc * T + gc0: mc * T + gc0 + 128]
                    e.tensor_scalar(evt5[:, 0:128], oacc[:, _ts(mc, 128)],
                                    bout_sb[:, mc:mc + 1], None, AL.add)
                    fin(e, e.tensor_tensor(xs, xs, evt5[:, 0:128], AL.add))
                B.step("vector", "sV", f_oe,
                       reads=("oacc", "bout", "evt", "x"),
                       writes=("x", "evt"))

    if 4 in sublayers or 6 in sublayers or 103 in sublayers:
        Cw = Carver()
        emit_wstats(Cw)
        emit_wquant()
    def d_dbg_f32():
        return dbg_f[:]
    if 103 in sublayers:
        B.new_epoch()
        C3 = Carver()
        wq_r = C3.fp8(4096)
        B.step("sync", "sLD",
               lambda e: fin(e, e.dma_start(out=wq_r[:, 0:4096], in_=sp_w1q[:, 0:4096])),
               reads=("spwq",), writes=("wqr",), extra=B.epoch)
        B.step("vector", "sV",
               lambda e: fin(e, e.tensor_copy(d_dbg_f32()[:, 0:4096], wq_r[:, 0:4096])),
               reads=("wqr",), writes=("dbg1",))
        B.step("sync", "sST",
               lambda e: fin(e, e.dma_start(out=dbg_d[:, 0:4096], in_=d_dbg_f32()[:, 0:4096])),
               reads=("dbg1",), writes=("dbgo",))
        B.step("sync", "sLD",
               lambda e: fin(e, e.dma_start(out=wq_r[:, 0:1024], in_=sp_w2q[:, 0:1024])),
               reads=("wqr",), writes=("wqr",))
        B.step("vector", "sV",
               lambda e: fin(e, e.tensor_copy(d_dbg_f32()[:, 0:1024], wq_r[:, 0:1024])),
               reads=("wqr",), writes=("dbg2",))
        def f_d3(e):
            sh = B.cur[0]
            e.dma_start(out=dbg_d[:, 4096:5120], in_=d_dbg_f32()[:, 0:1024]).then_inc(sh, 16)
            e.dma_start(out=dbg_d[:, 5120:5122], in_=mw_sb[:]).then_inc(sh, 16)
            e.dma_start(out=dbg_d[:, 5122:5124], in_=sw_sb[:]).then_inc(sh, 16)
            fin(e, e.dma_start(out=dbg_d[:, 5124:5136], in_=wsa[:]))
        B.step("sync", "sST", f_d3, reads=("dbg2", "mw", "sw"), writes=("dbgo",), n_dma=4)
    if 102 in sublayers:
        emit_l0ffn("dbg", phases=(1,))
    if 101 in sublayers:
        emit_l0ffn("s1", phases=(1,))
        B.new_epoch()
        B.step("gpsimd", "sGD",
               lambda e: fin(e, e.dma_start(out=out_d[:], in_=sp_h8[:, 0:NCH * T])),
               reads=("sp_h8",), writes=("out",), extra=B.epoch)
    if 1 in sublayers:
        emit_l0ffn("s1")
    if 2 in sublayers:
        emit_l0attn()
    if 3 in sublayers:
        emit_l0ffn("s3")
    if 4 in sublayers:
        emit_l1ffn("s4")
    if 5 in sublayers:
        emit_l1attn()
    if 6 in sublayers:
        emit_l1ffn("s6")

    def f_store2(e):
        inst = e.dma_start(out=out_d[:], in_=x_sb[:])
        fin(e, inst)
    st_sem, st_final = B.step("sync", "sST", f_store2, reads=("x",), writes=("out",))

    def f_wait(e):
        e.wait_ge(st_sem.h, st_final)
    B.items.append(("sync", [], f_wait, st_sem))

    with nc.Block() as blk:
        @blk.sync
        def _(e):
            B.replay("sync", e)

        @blk.vector
        def _(e):
            B.replay("vector", e)

        @blk.scalar
        def _(e):
            B.replay("scalar", e)

        @blk.tensor
        def _(e):
            B.replay("tensor", e)

        @blk.gpsimd
        def _(e):
            B.replay("gpsimd", e)

    st.close()
    return nc


def _prep_inputs(inputs):
    f32 = np.float32
    x = np.asarray(inputs["L0_attn_x"], f32)
    keep = _dropout_keep_mask(42, (4, 8192, 1024), 1.0 - 0.2)
    maskv = (keep.astype(f32) * f32(0.625)).astype(BFnp)

    tri = np.triu(np.ones((128, 128), f32)).astype(BFnp)
    ind = np.zeros((128, NCH * 16), f32)
    indT = np.zeros((16, NCH * 128), f32)
    for c in range(NCH):
        for p in range(128):
            h = (c * 128 + p) // 64
            ind[p, c * 16 + h] = 1.0
            indT[h, c * 128 + p] = 1.0

    def fm(v, k):
        return np.ascontiguousarray(np.asarray(v, f32).reshape(k, 128).T)

    shared = {
        "L0_attn_W_qkv": np.ascontiguousarray(inputs["L0_attn_W_qkv"], f32),
        "b_qkv_fm": fm(inputs["L0_attn_b_qkv"], 24),
        "b_qkv_row": np.ascontiguousarray(np.asarray(inputs["L0_attn_b_qkv"], f32).reshape(1, 3072)),
        "L0_attn_W_out": np.ascontiguousarray(inputs["L0_attn_W_out"], f32),
        "b_out_fm": fm(inputs["L0_attn_b_out"], 8),
        "W1_T": np.ascontiguousarray(np.asarray(inputs["L0_ffn_W1"], f32).T),
        "b1_fm": fm(inputs["L0_ffn_b1"], 32),
        "W2_T": np.ascontiguousarray(np.asarray(inputs["L0_ffn_W2"], f32).T),
        "b2_fm": fm(inputs["L0_ffn_b2"], 8),
        "L1_attn_Wq": np.ascontiguousarray(inputs["L1_attn_Wq"], f32),
        "L1_attn_Wk": np.ascontiguousarray(inputs["L1_attn_Wk"], f32),
        "L1_attn_Wv": np.ascontiguousarray(inputs["L1_attn_Wv"], f32),
        "L1_attn_Wg": np.ascontiguousarray(inputs["L1_attn_Wg"], f32),
        "L1_attn_Wo": np.ascontiguousarray(inputs["L1_attn_Wo"], f32),
        "gamma_fm": fm(inputs["L1_attn_gamma"], 8),
        "beta_fm": fm(inputs["L1_attn_beta"], 8),
        "w1L_T": np.ascontiguousarray(np.asarray(inputs["L1_ffn_w1"], f32).T),
        "b1L_fm": fm(inputs["L1_ffn_b1"], 32),
        "w2L_T": np.ascontiguousarray(np.asarray(inputs["L1_ffn_w2"], f32).T),
        "b2L_fm": fm(inputs["L1_ffn_b2"], 8),
        "tri": tri,
        "ident_f": np.eye(128, dtype=f32),
        "ind": ind,
        "indT": indT,
    }

    in_maps = []
    for core in range(8):
        b, half = core // 2, core % 2
        sl = slice(half * 4096, (half + 1) * 4096)
        xs = x[b, sl].T.reshape(NCH, 128, 4096).transpose(1, 0, 2).reshape(128, NCH * 4096)
        ms = maskv[b, sl].T.reshape(NCH, 128, 4096).transpose(1, 0, 2).reshape(128, NCH * 4096)
        m = dict(shared)
        m["x_fm"] = np.ascontiguousarray(xs)
        m["mask_fm"] = np.ascontiguousarray(ms)
        m["flag"] = np.full((128, 1), float(half), f32)
        in_maps.append(m)
    return in_maps


def kernel(**inputs):
    in_maps = _prep_inputs(inputs)
    nc = build_program()
    res = run_bass_kernel_spmd(nc, in_maps, core_ids=list(range(8)))
    out = np.empty((4, 8192, 1024), np.float32)
    for core in range(8):
        b, half = core // 2, core % 2
        o = np.asarray(res.results[core]["out_fm"])
        o = o.reshape(128, NCH, 4096).transpose(1, 0, 2).reshape(1024, 4096)
        out[b, half * 4096:(half + 1) * 4096] = o.T
    return out
